# version 14
# baseline (speedup 1.0000x reference)
import contextlib
import sys

import numpy as np

sys.path.insert(0, "/opt/trn_rl_repo")

import concourse.bass as bass
import concourse.mybir as mybir
from concourse.bass_utils import run_bass_kernel_spmd

F32 = mybir.dt.float32
BF16 = mybir.dt.bfloat16
AF = mybir.ActivationFunctionType
ALU = mybir.AluOpType
AX = mybir.AxisListType

B, T, D = 2, 4096, 2048
HK, HV, DK, DV = 16, 32, 128, 128
CH = 128
NCH = T // CH
SCALE = float(DK) ** -0.5
TSHARD = 1024
TT = 512
NTT = T // TT
CPT = TT // CH
HKC = 4
HVC = 8
E_EXP = 8
HEXP = 1408
HSH = 2816
NEG_BIG = -float(2.0 ** 30)

ENGINES = ("pe", "dve", "act", "pool", "sp")


class _Op:
    __slots__ = ("engine", "emit", "deps", "odeps", "idx", "inc_no", "dbg", "is_dma",
                 "cost", "aset", "prio", "sched_pos")

    def __init__(self, engine, emit, deps, idx, dbg="", is_dma=False, cost=None, aset=None):
        self.engine = engine
        self.emit = emit
        self.deps = deps
        self.odeps = []
        self.idx = idx
        self.inc_no = None
        self.dbg = dbg
        self.is_dma = is_dma
        self.cost = cost
        self.aset = aset
        self.prio = 0.0
        self.sched_pos = None


class _Chan:

    def __init__(self, queue, sem_name, inc=16):
        self.queue = queue
        self.sem_name = sem_name
        self.sem = None
        self.issued = 0
        self.inc = inc


class Sched:

    def __init__(self, nc):
        self.nc = nc
        self.ops = {e: [] for e in ENGINES}
        self.last_w = {}
        self.readers = {}
        self.chans = {}
        self._chan_rr = {}
        self.ctok = {}
        self._bar = {}
        self._bar_start = {e: 0 for e in ENGINES}
        self.n_emitted_waits = 0

    def add_chan_pool(self, queue, name, n):
        pool = [_Chan(queue, f"{name}{i}") for i in range(n)]
        self._chan_rr[name] = [pool, 0]
        return pool

    def next_chan(self, name):
        pool, i = self._chan_rr[name]
        self._chan_rr[name][1] = (i + 1) % len(pool)
        return pool[i]

    def _dep_tokens(self, reads, writes, engine):
        deps = []
        for r in reads:
            t = self.last_w.get(r)
            if t is not None:
                deps.append(t)
        for r in writes:
            t = self.last_w.get(r)
            if t is not None:
                deps.append(t)
            for tok in self.readers.get(r, {}).values():
                deps.append(tok)
        out, odep = [], []
        seen = set()
        for d in deps:
            if d in seen:
                continue
            seen.add(d)
            if d[0] == "e" and d[1] == "pe" and engine == "pe":
                odep.append(d)
            else:
                out.append(d)
        return out, odep

    def _record_access(self, token, reads, writes):
        for r in writes:
            self.last_w[r] = token
            self.readers[r] = {}
        for r in reads:
            d = self.readers.setdefault(r, {})
            d[token] = token

    def op(self, engine, emit, reads=(), writes=(), dbg="", cost=None, aset=None):
        deps, odeps = self._dep_tokens(reads, writes, engine)
        o = _Op(engine, emit, deps, len(self.ops[engine]), dbg, cost=cost, aset=aset)
        o.odeps = odeps
        if engine in self._bar:
            o.odeps.append(self._bar[engine])
        self.ops[engine].append(o)
        token = ("e", engine, o.idx)
        self._record_access(token, reads, writes)
        return token

    def dma(self, chan, emit, reads=(), writes=(), dbg="", cost=None):
        deps, odeps = self._dep_tokens(reads, writes, chan.queue)
        if chan.issued > 0:
            deps.append(("c", chan, chan.inc * chan.issued))
        chan.issued += 1
        total = chan.inc * chan.issued
        o = _Op(chan.queue, lambda eng, _e=emit, _c=chan: (
                _e(eng).then_inc(_c.sem, _c.inc) if _c.inc != 1 else _e(eng).then_inc(_c.sem)),
                deps, len(self.ops[chan.queue]), dbg, is_dma=True, cost=cost)
        o.odeps = odeps
        if chan.queue in self._bar:
            o.odeps.append(self._bar[chan.queue])
        self.ops[chan.queue].append(o)
        token = ("c", chan, total)
        self.ctok[(chan, total)] = o
        self._record_access(token, reads, writes)
        return token

    def barrier(self, keep=(), exclude=()):
        arrive = {}
        for e in ENGINES:
            o = _Op(e, lambda eng: eng.nop(nofuse=True), [], len(self.ops[e]), "bar_arr")
            o.odeps = [("e", e, i) for i in range(self._bar_start[e], o.idx)]
            self.ops[e].append(o)
            arrive[e] = ("e", e, o.idx)
        chan_lasts = []
        for pool, _ in self._chan_rr.values():
            for c in pool:
                if c.issued and c not in exclude:
                    chan_lasts.append(("c", c, c.inc * c.issued))
        for e in ENGINES:
            deps = [arrive[f] for f in ENGINES if f != e] + chan_lasts
            o = _Op(e, lambda eng: eng.nop(nofuse=True), deps, len(self.ops[e]), "barrier")
            o.odeps = [arrive[e]]
            self.ops[e].append(o)
            self._bar[e] = ("e", e, o.idx)
            self._bar_start[e] = len(self.ops[e])
        kept = {r: self.last_w[r] for r in keep if r in self.last_w}
        self.last_w = dict(kept)
        self.readers = {}

    DEF_COST = {"pe": 90, "dve": 330, "act": 420, "pool": 700, "sp": 250}
    DMA_ISSUE = {"pool": 900, "sp": 350}
    DMA_LAT = 1800

    def _tok_op(self, d):
        if d[0] == "e":
            return self.ops[d[1]][d[2]]
        return self.ctok[(d[1], d[2])]

    def schedule(self):
        import heapq
        allops = []
        for e in ENGINES:
            allops.extend(self.ops[e])
        nsucc = {}
        indeg = {}
        succs = {}
        for o in allops:
            indeg[id(o)] = 0
            succs[id(o)] = []
        for o in allops:
            for d in list(o.deps) + list(o.odeps):
                p = self._tok_op(d)
                succs[id(p)].append(o)
                indeg[id(o)] += 1
        outdeg = {id(o): len(succs[id(o)]) for o in allops}
        prio = {id(o): 0.0 for o in allops}
        stack = [o for o in allops if outdeg[id(o)] == 0]
        opmap = {id(o): o for o in allops}
        preds = {id(o): [self._tok_op(d) for d in list(o.deps) + list(o.odeps)] for o in allops}
        while stack:
            o = stack.pop()
            c = o.cost if o.cost is not None else (self.DMA_ISSUE[o.engine] + self.DMA_LAT
                                                  if o.is_dma else self.DEF_COST[o.engine])
            for p in preds[id(o)]:
                v = prio[id(o)] + c
                if v > prio[id(p)]:
                    prio[id(p)] = v
                outdeg[id(p)] -= 1
                if outdeg[id(p)] == 0:
                    stack.append(p)
        ready = {e: [] for e in ENGINES}
        seq = 0
        for o in allops:
            if indeg[id(o)] == 0:
                heapq.heappush(ready[o.engine], (-prio[id(o)], seq, o))
                seq += 1
        efree = {e: 0.0 for e in ENGINES}
        done = {}
        data_done = {}
        order = {e: [] for e in ENGINES}
        act_set = [None]
        n_remaining = len(allops)
        while n_remaining:
            best = None
            for e in ENGINES:
                if not ready[e]:
                    continue
                _, _, o = ready[e][0]
                rt = 0.0
                for d in list(o.deps) + list(o.odeps):
                    p = self._tok_op(d)
                    rt = max(rt, data_done[id(p)])
                st = max(efree[e], rt)
                key = (st, -prio[id(o)])
                if best is None or key < best[0]:
                    best = (key, e, o, st)
            _, e, o, st = best
            heapq.heappop(ready[e])
            occ = o.cost if (o.cost is not None and not o.is_dma) else (
                self.DMA_ISSUE[e] if o.is_dma else self.DEF_COST[e])
            if e == "act" and o.aset is not None:
                if act_set[0] is not None and act_set[0] != o.aset:
                    occ += 2700.0
                act_set[0] = o.aset
            fin = st + occ
            efree[e] = fin
            done[id(o)] = fin
            data_done[id(o)] = fin + ((o.cost if o.cost is not None else 0) + self.DMA_LAT
                                      if o.is_dma else 0.0)
            order[e].append(o)
            o.sched_pos = len(order[e]) - 1
            n_remaining -= 1
            for s in succs[id(o)]:
                indeg[id(s)] -= 1
                if indeg[id(s)] == 0:
                    heapq.heappush(ready[s.engine], (-prio[id(s)], seq, s))
                    seq += 1
        self.est_span = max(done.values())
        return order

    def finalize(self, block, sems, chan_sems):
        order = self.schedule()
        for e in ENGINES:
            for o in self.ops[e]:
                for d in o.deps:
                    if d[0] == "e":
                        t = self.ops[d[1]][d[2]]
                        assert not t.is_dma, f"engine dep targets DMA op {t.dbg}"
                        if t.dbg != "bar_arr":
                            t.inc_no = True
        arr_carrier = {}
        for e in ENGINES:
            last_real = None
            for o in order[e]:
                if o.dbg == "bar_arr":
                    arr_carrier[(e, o.idx)] = last_real
                    if last_real is not None:
                        last_real.inc_no = True
                elif not o.is_dma and o.dbg != "barrier":
                    last_real = o
        inc_val = {}
        arr_val = {}
        for e in ENGINES:
            n = 0
            for o in order[e]:
                if o.dbg == "bar_arr":
                    car = arr_carrier[(e, o.idx)]
                    arr_val[(e, o.idx)] = inc_val[(e, car.idx)] if car is not None else 0
                if o.inc_no:
                    n += 1
                    inc_val[(e, o.idx)] = n
        self.inc_val = inc_val
        self.arr_val = arr_val
        for name, (pool, _) in self._chan_rr.items():
            for c in pool:
                c.sem = chan_sems[c.sem_name]

        def run_engine(ename):
            sem_self = sems[ename]

            def body(eng):
                vclock = {}
                for o in order[ename]:
                    for d in o.deps:
                        if d[0] == "e":
                            key = ("e", d[1])
                            t = self.ops[d[1]][d[2]]
                            val = (arr_val[(d[1], d[2])] if t.dbg == "bar_arr"
                                   else inc_val[(d[1], d[2])])
                            tgt = sems[d[1]]
                        else:
                            key = ("c", d[1].sem_name)
                            val = d[2]
                            tgt = d[1].sem
                        if vclock.get(key, 0) >= val:
                            continue
                        eng.wait_ge(tgt, val)
                        self.n_emitted_waits += 1
                        vclock[key] = val
                    ins = o.emit(eng)
                    if o.inc_no:
                        ins.then_inc(sem_self, 1)
            return body

        for e in ENGINES:
            name = {"pe": "tensor", "dve": "vector", "act": "scalar",
                    "pool": "gpsimd", "sp": "sync"}[e]
            getattr(block, name)(run_engine(e))

    def stats(self):
        return {e: len(self.ops[e]) for e in ENGINES}



def build_program(debug=False):
    nc = bass.Bass("TRN2", target_bir_lowering=False)

    def inp(name, shape):
        return nc.dram_tensor(name, list(shape), F32, kind="ExternalInput").ap()

    x_d = inp("x_sh", (TSHARD, D))
    wq_d = inp("wq_sl", (D, 3072))
    wba_d = inp("wba_sl", (D, 16))
    convw_d = inp("convw_sl", (128, 64))
    dtb_d = inp("dtb_sl", (1, 8))
    alog_d = inp("alog_sl", (1, 8))
    normw_d = inp("normw", (128, 1))
    wout_d = inp("wout", (HV * DV, D))
    rw_d = inp("router_w", (8, D))
    wg_d = inp("wg", (E_EXP, D, HEXP))
    wu_d = inp("wu", (E_EXP, D, HEXP))
    wd_d = inp("wd", (E_EXP, HEXP, D))
    sg_d = inp("sg", (D, HSH))
    su_d = inp("su", (D, HSH))
    sd_d = inp("sd", (HSH, D))
    iden128_d = inp("c_iden128", (128, 128))
    ones_d = inp("c_ones", (128, 128))
    sel_d = inp("c_sel", (128, 256))
    tril128_d = inp("c_tril128", (128, 128))
    mask128_d = inp("c_mask128", (128, 128))
    strict128_d = inp("c_strict128", (128, 128))
    cbias_d = inp("c_bias", (128, 5))

    out_d = nc.dram_tensor("out_sh", [TSHARD, D], F32, kind="ExternalOutput").ap()

    xn_con = nc.dram_tensor("xn_con", [D, TSHARD], BF16).ap()
    xn_all = nc.dram_tensor("xn_all", [4 * D, TSHARD], BF16).ap()
    xg_con = nc.dram_tensor("xg_con", [T, HVC * DV], BF16).ap()
    xg_all_h = nc.dram_tensor("xg_all", [4 * T, HVC * DV], BF16)
    xg_all = xg_all_h.ap()
    rank_reg = nc.gpsimd.alloc_register("rank_reg")
    roff_reg = nc.gpsimd.alloc_register("roff_reg")
    tmp_reg = nc.gpsimd.alloc_register("tmp_reg")
    y_dram = nc.dram_tensor("y_dram", [TSHARD, D], F32).ap()
    dbg_qts = nc.dram_tensor("dbg_qts", [128, 2048], BF16).ap() if debug else None
    dbg_kts = nc.dram_tensor("dbg_kts", [128, 2048], BF16).ap() if debug else None
    dbg_ob = nc.dram_tensor("dbg_ob", [128, 4096], F32).ap() if debug else None
    dbg_u = nc.dram_tensor("dbg_u", [128, 4096], BF16).ap() if debug else None
    dbg_E = nc.dram_tensor("dbg_E", [128, 4096], F32).ap() if debug else None
    dbg_r = nc.dram_tensor("dbg_r", [128, 4096], BF16).ap() if debug else None
    dbg_S = nc.dram_tensor("dbg_S", [128, 1024], BF16).ap() if debug else None
    dbg_Ss = nc.dram_tensor("dbg_Ss", [128, 1024], F32).ap() if debug else None
    dbg_kd = nc.dram_tensor("dbg_kd", [128, 1024], BF16).ap() if debug else None
    dbg_vts = nc.dram_tensor("dbg_vts", [128, 4096], BF16).ap() if debug else None
    dbg_zts = nc.dram_tensor("dbg_zts", [128, 4096], BF16).ap() if debug else None

    sched = Sched(nc)
    ch_w = sched.add_chan_pool("pool", "cw", 6)
    ch_y = sched.add_chan_pool("pool", "cy", 4)
    ch_s = sched.add_chan_pool("sp", "cs", 6)
    ch_x = sched.add_chan_pool("pool", "cx", 2)
    cc1 = _Chan("pool", "cc1", inc=1)
    cc2 = _Chan("pool", "cc2", inc=1)
    sched._chan_rr["cc1"] = [[cc1], 0]
    sched._chan_rr["cc2"] = [[cc2], 0]

    es = contextlib.ExitStack()
    with es:
        def sb(name, p, cols, dt):
            return es.enter_context(nc.sbuf_tensor(name, [p, cols], dt))

        iden128 = sb("iden128", 128, 128, BF16)
        onesb = sb("onesb", 128, 64, BF16)
        onesf = sb("onesf", 128, 128, F32)
        selb = sb("selb", 128, 256, BF16)
        tril128 = sb("tril128", 128, 128, F32)
        mask128 = sb("mask128", 128, 128, BF16)
        strict128 = sb("strict128", 128, 128, F32)
        cstc = sb("cstc", 128, 5, F32)
        ln_scr = sb("ln_scr", 128, 8, F32)
        rsc = sb("rsc", 128, 48, F32)
        convw_sb = sb("convw_sb", 128, 64, F32)
        dtb_sb = sb("dtb_sb", 1, 8, F32)
        alog_sb = sb("alog_sb", 1, 8, F32)
        nrow_sb = sb("nrow_sb", 1, 8, F32)
        dtb8 = sb("dtb8", 128, 8, F32)
        ea8 = sb("ea8", 128, 8, F32)
        normw_sb = sb("normw_sb", 128, 1, F32)
        gates_sb = sb("gates_sb", 128, 64, F32)
        ba_tok = sb("ba_tok", 128, 512, F32)
        b8 = sb("b8", 128, 256, F32)
        a8 = sb("a8", 128, 256, F32)
        beta_tok = sb("beta_tok", 128, 256, F32)
        g_tok = sb("g_tok", 128, 256, F32)
        G_tok = sb("G_tok", 128, 256, F32)
        rows2 = [sb(f"rows2_{i}", 128, 4096, BF16) for i in range(2)]
        negG_sl = sb("negG_sl", 128, 256, F32)
        eG8_sl = sb("eG8_sl", 128, 256, F32)
        neG_sl = sb("neG_sl", 128, 256, F32)
        beta_sl = sb("beta_sl", 128, 256, F32)
        betaS_sl = sb("betaS_sl", 128, 256, F32)
        egC_all = sb("egC_all", 128, 512, F32)
        gCrow = sb("gCrow", 1, 512, F32)
        S_sb = sb("S_sb", 128, HVC * DV, F32)
        S_bf = sb("S_bf", 128, HVC * DV, BF16)
        ssqQ = sb("ssqQ", 128, 32, F32)
        ssqK = sb("ssqK", 128, 32, F32)
        rsqQ = sb("rsqQ", 128, 32, F32)
        rsqK = sb("rsqK", 128, 32, F32)
        msq = [sb(f"msq{i}", 128, 32, F32) for i in range(2)]
        rstd = [sb(f"rstd{i}", 128, 32, F32) for i in range(2)]
        wba_sb = sb("wba_sb", 128, 256, BF16)
        rwT = sb("rwT", 128, 128, BF16)
        AB = sb("AB", 128, 65536, BF16)
        AFA = sb("AFA", 128, 4864, F32)

        psf = [es.enter_context(nc.psum_tensor(f"psf{i}", [128, 512], F32)) for i in range(6)]
        psb = [es.enter_context(nc.psum_tensor(f"psb{i}", [128, 1024], BF16)) for i in range(2)]

        sems = {e: es.enter_context(nc.semaphore(f"sem_{e}")) for e in ENGINES}
        chan_sems = {}
        for pool_, _ in sched._chan_rr.values():
            for c in pool_:
                chan_sems[c.sem_name] = es.enter_context(nc.semaphore(c.sem_name))

        ctx = dict(locals())
        _build_body(sched, ctx)

        with nc.Block() as block:
            sched.finalize(block, sems, chan_sems)

    return nc, sched


class _Carve:
    def __init__(self, cols):
        self.cols = cols
        self.off = 0

    def take(self, n, align=64):
        o = (self.off + align - 1) // align * align
        assert o + n <= self.cols, f"arena overflow: {o}+{n} > {self.cols}"
        self.off = o + n
        return o


def _build_body(sched, C):
    from types import SimpleNamespace
    g = SimpleNamespace(**C)
    nc = g.nc
    op, dma, nch = sched.op, sched.dma, sched.next_chan

    def _rowslot(hv):
        if hv < 6:
            return 0, 32 * (hv // 2), hv % 2
        return 1, 0, hv - 6
    AB, AFW = g.AB, g.AFA
    psf, psb = g.psf, g.psb

    def cast_load(dst_ap, src_ap, rg, rr=(), dbg=""):
        return dma(nch("cw"), lambda e, d=dst_ap, s=src_ap: e.dma_start(out=d, in_=s),
                   reads=rr, writes=[rg], dbg=dbg)

    def plain_load(dst_ap, src_ap, rg, rr=(), dbg=""):
        return dma(nch("cs"), lambda e, d=dst_ap, s=src_ap: e.dma_start(out=d, in_=s),
                   reads=rr, writes=[rg], dbg=dbg)

    cast_load(g.iden128[:, :], g.iden128_d, "iden128")
    cast_load(g.onesb[:, :], g.ones_d[:, 0:64], "onesb")
    plain_load(g.onesf[:, :], g.ones_d, "onesf")
    cast_load(g.selb[:, :], g.sel_d, "selb")
    plain_load(g.tril128[:, :], g.tril128_d, "tril128")
    cast_load(g.mask128[:, :], g.mask128_d, "mask128")
    plain_load(g.strict128[:, :], g.strict128_d, "strict128")
    plain_load(g.cstc[:, :], g.cbias_d, "cstc")
    plain_load(g.convw_sb[:, :], g.convw_d, "convw")
    plain_load(g.dtb_sb[:, :], g.dtb_d, "dtb")
    plain_load(g.alog_sb[:, :], g.alog_d, "alog")
    plain_load(g.normw_sb[:, :], g.normw_d, "normw")
    cast_load(g.wba_sb[:, :].rearrange("p (c n) -> p c n", c=16),
              g.wba_d.rearrange("(c p) n -> p c n", p=128), "wba")

    cab = _Carve(AB.shape[1])
    caf = _Carve(AFW.shape[1])
    o_xnl = cab.take(16 * TSHARD)
    o_xt = [caf.take(D), caf.take(D)]
    o_xnbf = [cab.take(D), cab.take(D)]
    o_junk = cab.take(D)
    o_rwt = cab.take(D)
    o_ghi = cab.take(4096)
    o_glo = cab.take(4096)

    xnl = AB[:, o_xnl:o_xnl + 16 * TSHARD]
    lns = g.ln_scr

    for u in range(8):
        xt = AFW[:, o_xt[u % 2]:o_xt[u % 2] + D]
        rx = f"xt{u % 2}"
        plain_load(xt, g.x_d[128 * u:128 * u + 128, :], rx, dbg=f"ldx{u}")
        sc = lns[:, 4 * (u % 2):4 * (u % 2) + 4]
        rs = f"lns{u % 2}"
        op("dve", lambda e, a=sc[:, 0:1], b=xt: e.tensor_reduce(a, b, AX.X, ALU.add, False, False, True),
           reads=[rx], writes=[rs], dbg=f"lnsum{u}")
        op("dve", lambda e, a=sc[:, 0:1]: e.tensor_scalar(a, a, 1.0 / D, None, ALU.mult),
           reads=[rs], writes=[rs])
        op("dve", lambda e, a=xt, m=sc[:, 0:1]: e.tensor_scalar(a, a, m[:, 0:1], None, ALU.add),
           reads=[rx, rs], writes=[rx])
        op("dve", lambda e, jk=AB[:, o_junk:o_junk + D], a=xt, s=sc[:, 1:2]:
           e.scalar_tensor_tensor(jk, a, 1.0, a, ALU.mult, ALU.mult, accum_out=s),
           reads=[rx], writes=["junk", rs])
        op("act", lambda e, s=sc[:, 1:2], t=sc[:, 2:3]: e.activation(t, s, AF.Ln, bias=g.cstc[:, 0:1], scale=1.0 / D),
           reads=[rs, "cstc"], writes=[rs])
        op("act", lambda e, t=sc[:, 2:3]: e.activation(t, t, AF.Exp, bias=0.0, scale=-0.5),
           reads=[rs], writes=[rs])
        xnb = AB[:, o_xnbf[u % 2]:o_xnbf[u % 2] + D]
        rxn = f"xnbf{u % 2}"
        op("dve", lambda e, a=xnb, b=xt, s=sc[:, 2:3]: e.tensor_scalar(a, b, s[:, 0:1], None, ALU.mult),
           reads=[rx, rs], writes=[rxn])
        for half in range(2):
            pb = psb[half]
            for d8 in range(8):
                dd = 8 * half + d8
                op("pe", lambda e, o=pb[:, 128 * d8:128 * d8 + 128], i=xnb[:, 128 * dd:128 * dd + 128],
                   idn=g.iden128[:, :]: e.transpose(o, i, idn),
                   reads=[rxn, "iden128"], writes=[f"psb{half}"], dbg=f"xnT{u}.{dd}")
            out_ap = AB[:, o_xnl:o_xnl + 16384].rearrange("p (d t) -> p d t", d=16)[
                :, 8 * half:8 * half + 8, 128 * u:128 * u + 128]
            op("dve", lambda e, a=out_ap, b=pb[:, :]: e.tensor_copy(a, b.rearrange("p (d t) -> p d t", d=8)),
               reads=[f"psb{half}"], writes=["xnl"], dbg=f"xnTe{u}.{half}")

    dma(nch("cs"), lambda e: e.dma_start(
        out=g.xn_con.rearrange("(c p) t -> p c t", p=128),
        in_=AB[:, o_xnl:o_xnl + 16384].rearrange("p (c t) -> p c t", c=16)),
        reads=["xnl"], writes=["xn_con"], dbg="st_xncon")
    dma(g.cc1, lambda e: e.collective_compute(
        "AllGather", ALU.bypass,
        replica_groups=[[0, 1, 2, 3], [4, 5, 6, 7]],
        ins=[g.xn_con.opt()], outs=[g.xn_all.opt()]),
        reads=["xn_con"], writes=["xn_all"], dbg="AG1", cost=60000)

    rwt = AB[:, o_rwt:o_rwt + D]
    cast_load(rwt[0:8, :], g.rw_d, "rwtmp")
    for d16 in range(16):
        op("pe", lambda e, o=psb[0][:, 8 * d16:8 * d16 + 8], i=rwt[0:8, 128 * d16:128 * d16 + 128],
           idn=g.iden128[0:8, 0:8]: e.transpose(o, i, idn),
           reads=["rwtmp", "iden128"], writes=["psb0"], dbg=f"rwT{d16}")
    op("dve", lambda e: e.tensor_copy(g.rwT[:, :], psb[0][:, 0:128]),
       reads=["psb0"], writes=["rwT"])

    for u in range(8):
        pg = psf[0][:, 0:8]
        for d16 in range(16):
            op("pe", lambda e, p=pg, l=AB[:, o_xnl + 1024 * d16 + 128 * u:o_xnl + 1024 * d16 + 128 * u + 128],
               r=g.rwT[:, 8 * d16:8 * d16 + 8], s=(d16 == 0), t=(d16 == 15):
               e.matmul(p, l, r, start=s, stop=t),
               reads=["xnl", "rwT"], writes=["psf0"], dbg=f"rmm{u}.{d16}")
        sc = g.rsc
        rg = "rsc"
        op("dve", lambda e, p=pg: e.tensor_reduce(sc[:, 8:9], p, AX.X, ALU.max, False, False, True),
           reads=["psf0"], writes=[rg])
        op("act", lambda e, p=pg: e.activation(sc[:, 0:8], p, AF.Exp, bias=sc[:, 8:9], scale=1.0),
           reads=["psf0", rg], writes=[rg + "e"])
        op("dve", lambda e: e.tensor_reduce(sc[:, 9:10], sc[:, 0:8], AX.X, ALU.add),
           reads=[rg + "e"], writes=[rg + "s"])
        op("dve", lambda e: e.reciprocal(sc[:, 10:11], sc[:, 9:10]),
           reads=[rg + "s"], writes=[rg + "s"])
        op("dve", lambda e: e.tensor_scalar(sc[:, 0:8], sc[:, 0:8], sc[:, 10:11], None, ALU.mult),
           reads=[rg + "e", rg + "s"], writes=[rg + "e"])
        op("dve", lambda e: e.tensor_reduce(sc[:, 11:12], sc[:, 0:8], AX.X, ALU.max),
           reads=[rg + "e"], writes=[rg + "m"])
        op("dve", lambda e: e.tensor_scalar(sc[:, 16:24], sc[:, 0:8], sc[:, 11:12], None, ALU.is_lt),
           reads=[rg + "e", rg + "m"], writes=[rg + "t"])
        op("dve", lambda e: e.tensor_tensor(sc[:, 16:24], sc[:, 16:24], sc[:, 0:8], ALU.mult),
           reads=[rg + "t", rg + "e"], writes=[rg + "t"])
        op("dve", lambda e: e.tensor_reduce(sc[:, 12:13], sc[:, 16:24], AX.X, ALU.max),
           reads=[rg + "t"], writes=[rg + "m2"])
        op("dve", lambda e: e.tensor_scalar(sc[:, 24:32], sc[:, 0:8], sc[:, 12:13], None, ALU.is_ge),
           reads=[rg + "e", rg + "m2"], writes=[rg + "k"])
        op("dve", lambda e: e.tensor_tensor(sc[:, 24:32], sc[:, 24:32], sc[:, 0:8], ALU.mult),
           reads=[rg + "k", rg + "e"], writes=[rg + "k"])
        op("dve", lambda e: e.tensor_reduce(sc[:, 13:14], sc[:, 24:32], AX.X, ALU.add),
           reads=[rg + "k"], writes=[rg + "s2"])
        op("dve", lambda e: e.tensor_scalar(sc[:, 13:14], sc[:, 13:14], 1e-20, None, ALU.add),
           reads=[rg + "s2"], writes=[rg + "s2"])
        op("dve", lambda e: e.reciprocal(sc[:, 14:15], sc[:, 13:14]),
           reads=[rg + "s2"], writes=[rg + "s2"])
        op("dve", lambda e, u_=u: e.tensor_scalar(g.gates_sb[:, 8 * u_:8 * u_ + 8], sc[:, 24:32],
                                                  sc[:, 14:15], None, ALU.mult),
           reads=[rg + "k", rg + "s2"], writes=["gates"], dbg=f"gate{u}")

    o_bat = [cab.take(2048), cab.take(2048)]
    for w in range(32):
        bt = AB[:, o_bat[w % 2]:o_bat[w % 2] + 2048]
        rb = f"bat{w % 2}"
        rb_blk = w // 8
        dma(nch("cs"), lambda e, d=bt.rearrange("p (c t) -> p c t", c=16), w_=w, rbk=rb_blk:
            e.dma_start(out=d, in_=g.xn_all[2048 * rbk:2048 * rbk + 2048,
                                            128 * (w_ % 8):128 * (w_ % 8) + 128].rearrange(
                "(c p) t -> p c t", p=128)),
            reads=["xn_all"], writes=[rb], dbg=f"ld_bat{w}")
        pg = psf[1][:, 0:16]
        for d16 in range(16):
            op("pe", lambda e, p=pg, l=bt[:, 128 * d16:128 * d16 + 128],
               r=g.wba_sb[:, 16 * d16:16 * d16 + 16], s=(d16 == 0), t=(d16 == 15):
               e.matmul(p, l, r, start=s, stop=t),
               reads=[rb, "wba"], writes=["psf1"], dbg=f"bamm{w}.{d16}")
        op("dve", lambda e, w_=w, p=pg: e.tensor_copy(g.ba_tok[:, 16 * w_:16 * w_ + 16], p),
           reads=["psf1"], writes=["ba_tok"], dbg=f"bae{w}")

    bav = g.ba_tok[:, :].rearrange("p (w h c) -> p w h c", w=32, c=4)
    op("dve", lambda e: e.tensor_copy(g.b8[:, :].rearrange("p (w h c) -> p w h c", w=32, c=2),
                                      bav[:, :, :, 0:2]), reads=["ba_tok"], writes=["b8"])
    op("dve", lambda e: e.tensor_copy(g.a8[:, :].rearrange("p (w h c) -> p w h c", w=32, c=2),
                                      bav[:, :, :, 2:4]), reads=["ba_tok"], writes=["a8"])
    op("act", lambda e: e.activation(g.beta_tok[:, :], g.b8[:, :], AF.Exp, bias=0.0, scale=-1.0),
       reads=["b8"], writes=["beta"])
    op("dve", lambda e: e.tensor_scalar(g.beta_tok[:, :], g.beta_tok[:, :], 1.0, None, ALU.add),
       reads=["beta"], writes=["beta"])
    op("dve", lambda e: e.reciprocal(g.beta_tok[:, :], g.beta_tok[:, :]),
       reads=["beta"], writes=["beta"])
    op("act", lambda e: e.activation(g.nrow_sb[:, :], g.alog_sb[:, :], AF.Exp),
       reads=["alog"], writes=["nrow"])
    op("dve", lambda e: e.tensor_scalar(g.nrow_sb[:, :], g.nrow_sb[:, :], -1.0, None, ALU.mult),
       reads=["nrow"], writes=["nrow"])
    op("pe", lambda e: e.matmul(psf[2][:, 0:8], g.onesf[0:1, :], g.dtb_sb[:, :], start=True, stop=True),
       reads=["onesf", "dtb"], writes=["psf2"])
    op("dve", lambda e: e.tensor_copy(g.dtb8[:, :], psf[2][:, 0:8]), reads=["psf2"], writes=["dtb8"])
    op("pe", lambda e: e.matmul(psf[2][:, 8:16], g.onesf[0:1, :], g.nrow_sb[:, :], start=True, stop=True),
       reads=["onesf", "nrow"], writes=["psf2"])
    op("dve", lambda e: e.tensor_copy(g.ea8[:, :], psf[2][:, 8:16]), reads=["psf2"], writes=["ea8"])
    a8v = g.a8[:, :].rearrange("p (w h) -> p w h", h=8)
    op("dve", lambda e: e.tensor_tensor(a8v, a8v, g.dtb8[:, :].unsqueeze(1).broadcast_to([128, 32, 8]),
                                        ALU.add), reads=["a8", "dtb8"], writes=["a8"])
    scA = AFW[:, 4096:4352]
    scB = AFW[:, 4352:4608]
    scC = AFW[:, 4608:4864]
    op("act", lambda e: e.activation(scA, g.a8[:, :], AF.Abs), reads=["a8"], writes=["spA", "xt0"])
    op("act", lambda e: e.activation(scA, scA, AF.Exp, bias=0.0, scale=-1.0), reads=["spA"], writes=["spA"])
    op("act", lambda e: e.activation(scB, scA, AF.Ln, bias=g.cstc[:, 2:3]), reads=["spA", "cstc"], writes=["spB"])
    op("dve", lambda e: e.tensor_scalar(scC, g.a8[:, :], 0.0, None, ALU.max), reads=["a8"], writes=["spC", "xt0"])
    op("dve", lambda e: e.tensor_tensor(scB, scB, scC, ALU.add), reads=["spB", "spC"], writes=["spB"])
    gv = g.g_tok[:, :].rearrange("p (w h) -> p w h", h=8)
    op("dve", lambda e: e.tensor_tensor(gv, scB.rearrange("p (w h) -> p w h", h=8),
                                        g.ea8[:, :].unsqueeze(1).broadcast_to([128, 32, 8]), ALU.mult),
       reads=["spB", "ea8"], writes=["g_tok"])

    G_rows = AFW[0:8, 0:4096]
    for c in range(NCH):
        pk = psf[3 + (c % 2)]
        rps = f"psf{3 + c % 2}"
        op("pe", lambda e, p=pk[:, 0:8], r=g.g_tok[:, 8 * c:8 * c + 8]:
           e.matmul(p, g.tril128[:, :], r, start=True, stop=True),
           reads=["g_tok", "tril128"], writes=[rps], dbg=f"Gc{c}")
        op("dve", lambda e, c_=c, p=pk: e.tensor_copy(g.G_tok[:, 8 * c_:8 * c_ + 8], p[:, 0:8]),
           reads=[rps], writes=["G_tok"], dbg=f"Ge{c}")
        op("pe", lambda e, p=pk[0:8, 128:256], l=g.g_tok[:, 8 * c:8 * c + 8]:
           e.matmul(p, l, g.tril128[:, :], start=True, stop=True),
           reads=["g_tok", "tril128"], writes=[rps], dbg=f"Gr{c}")
        op("dve", lambda e, c_=c, p=pk: e.tensor_copy(G_rows[:, 128 * c_:128 * c_ + 128], p[0:8, 128:256]),
           reads=[rps], writes=(["G_rows", "xt0", "xt1"] if c == 0 else ["G_rows"]),
           dbg=f"Gre{c}")
    GhiR = AB[0:8, o_ghi:o_ghi + 4096]
    GloR = AB[0:8, o_glo:o_glo + 4096]
    op("dve", lambda e: e.tensor_copy(GhiR, G_rows), reads=["G_rows"], writes=["GhiR"])
    op("dve", lambda e: e.tensor_tensor(GloR, G_rows, GhiR, ALU.subtract),
       reads=["G_rows", "GhiR"], writes=["GloR"])
    for hv in range(8):
        tid, gb, sl_ = _rowslot(hv)
        rr = gb + 2 * sl_
        dma(nch("cs"), lambda e, hv_=hv, t2=tid, r2=rr: e.dma_start(
            out=g.rows2[t2][r2:r2 + 1, :], in_=GhiR[hv_:hv_ + 1, :]),
            reads=["GhiR"], writes=["rows2"], dbg=f"rh{hv}")
        dma(nch("cs"), lambda e, hv_=hv, t2=tid, r2=rr: e.dma_start(
            out=g.rows2[t2][r2 + 1:r2 + 2, :], in_=GloR[hv_:hv_ + 1, :]),
            reads=["GloR"], writes=["rows2"], dbg=f"rl{hv}")
    scr2 = AFW[:, 4352:4608]
    op("dve", lambda e: e.tensor_scalar(g.negG_sl[:, :], g.G_tok[:, :], -1.0, None, ALU.mult),
       reads=["G_tok"], writes=["negG_sl"])
    op("act", lambda e: e.activation(scr2, g.G_tok[:, :], AF.Exp),
       reads=["G_tok"], writes=["scr2", "spB"])
    op("dve", lambda e: e.tensor_scalar(g.eG8_sl[:, :], scr2, SCALE, None, ALU.mult),
       reads=["scr2"], writes=["eG8_sl"])
    op("dve", lambda e: e.tensor_scalar(g.neG_sl[:, :], scr2, -1.0, None, ALU.mult),
       reads=["scr2"], writes=["neG_sl"])
    op("dve", lambda e: e.tensor_scalar(g.betaS_sl[:, :], g.beta_tok[:, :], SCALE, None, ALU.mult),
       reads=["beta"], writes=["betaS_sl"])
    dma(nch("cs"), lambda e: e.dma_start(out=g.gCrow[0:1, 0:256], in_=g.G_tok[127:128, :]),
        reads=["G_tok"], writes=["gCrow"], dbg="gC")
    op("pe", lambda e: e.matmul(psf[2][:, 0:256], g.onesf[0:1, :], g.gCrow[0:1, 0:256],
                                start=True, stop=True),
       reads=["onesf", "gCrow"], writes=["psf2"], dbg="egCmm")
    op("act", lambda e: e.activation(g.egC_all[:, 0:256], psf[2][:, 0:256], AF.Exp),
       reads=["psf2"], writes=["egC"], dbg="egCexp")

    sched.barrier()
    cab = _Carve(AB.shape[1])
    caf = _Carve(AFW.shape[1])
    o_wq = [cab.take(2048) for _ in range(3)]
    o_rhs = cab.take(8192)
    o_cv = cab.take(16 * 520)
    o_ms = [cab.take(512), cab.take(512)]
    o_tt = [cab.take(512), cab.take(512)]
    o_qts = [cab.take(2048), cab.take(2048)]
    o_kts = [cab.take(2048), cab.take(2048)]
    o_vts = cab.take(4096)
    o_zts = cab.take(4096)
    o_qh = [cab.take(512), cab.take(512)]
    o_kh = [cab.take(512), cab.take(512)]
    o_kd = [cab.take(1024), cab.take(1024)]
    o_NT = [cab.take(1024) for _ in range(6)]
    o_N = [cab.take(1024), cab.take(1024)]
    o_A = [cab.take(1024), cab.take(1024)]
    o_PE = cab.take(2048)
    o_yb = [cab.take(1024), cab.take(1024)]
    o_xg = cab.take(4096)
    o_jk = cab.take(1024)
    o_jk2 = cab.take(128)
    o_E = [caf.take(1024), caf.take(1024)]
    o_ob = caf.take(1024)
    o_oq = caf.take(1024)
    o_kc = caf.take(32)

    def ab(off, n):
        return AB[:, off:off + n]

    def af(off, n):
        return AFW[:, off:off + n]

    def nt_buf(k):
        return (ab(o_NT[k], 1024), f"NT_{k}") if k < 6 else (ab(o_PE, 1024), "PEX")

    for hv in range(HVC):
        op("dve", lambda e, h_=hv: e.memset(g.S_sb[:, 128 * h_:128 * h_ + 128], 0.0),
           reads=[], writes=[f"Ss{hv}"], dbg=f"S0.{hv}")
        op("dve", lambda e, h_=hv: e.memset(g.S_bf[:, 128 * h_:128 * h_ + 128], 0.0),
           reads=[], writes=[f"Sb{hv}"], dbg=f"Sbf0.{hv}")
    for ct in range(16):
        op("dve", lambda e, c_=ct: e.memset(ab(o_cv + 520 * c_, 3), 0.0),
           reads=[], writes=[f"cv{ct}"], dbg=f"cv0.{ct}")

    def gen_head(ti, part):
        par = ti % 2
        rrhs = "rhs"
        rhsb = ab(o_rhs, 8192)
        if part == 0:
            dma(nch("cs"), lambda e, t_=ti, d=rhsb.rearrange("p (c t) -> p c t", c=16): e.dma_start(
                out=d, in_=g.xn_all[2048 * (t_ // 2):2048 * (t_ // 2) + 2048,
                                    512 * (t_ % 2):512 * (t_ % 2) + 512].rearrange("(c p) t -> p c t", p=128)),
                reads=["xn_all"], writes=[rrhs], dbg=f"ldrhs{ti}", cost=6000)
        for ct in (range(8) if part == 0 else range(8, 24)):
            cpar = ct % 2
            rwq = f"wqt{ct % 3}"
            wqb = ab(o_wq[ct % 3], 2048)
            dma(nch("cw"), lambda e, c_=ct, d=wqb.rearrange("p (c n) -> p c n", c=16): e.dma_start(
                out=d, in_=g.wq_d[:, 128 * c_:128 * c_ + 128].rearrange("(c p) n -> p c n", p=128)),
                reads=[], writes=[rwq], dbg=f"ldwq{ti}.{ct}", cost=3000)
            pq = psf[cpar]
            rps = f"psf{cpar}"
            for d16 in range(16):
                op("pe", lambda e, p=pq[:, 0:512], l=wqb[:, 128 * d16:128 * d16 + 128],
                   r=rhsb.rearrange("p (c t) -> p c t", c=16)[:, d16, :],
                   s=(d16 == 0), t=(d16 == 15): e.matmul(p, l, r, start=s, stop=t),
                   reads=[rwq, rrhs], writes=[rps], dbg=f"qmm{ti}.{ct}.{d16}", cost=230)
            tt_ = ab(o_tt[ct % 2], 512)
            rt = f"tt{ct % 2}"
            ms = ab(o_ms[ct % 2], 512)
            rm = f"ms{ct % 2}"
            if ct < 16:
                rcv = f"cv{ct}"
                cv = ab(o_cv + 520 * ct, 520)
                if ti > 0:
                    op("dve", lambda e, c=cv: e.tensor_copy(c[:, 0:3], c[:, 512:515]),
                       reads=[rcv], writes=[rcv], dbg=f"hist{ti}.{ct}")
                op("dve", lambda e, c=cv, p=pq: e.tensor_copy(c[:, 3:515], p[:, 0:512]),
                   reads=[rps], writes=[rcv], dbg=f"cve{ti}.{ct}")
                op("act", lambda e, o=ms, i=cv[:, 0:512], w=g.convw_sb[:, 4 * ct:4 * ct + 1]:
                   e.activation(o, i, AF.Identity, bias=0.0, scale=w[:, 0:1]),
                   reads=[rcv, "convw"], writes=[rm], dbg=f"cnv0.{ti}.{ct}")
                for j in (1, 3):
                    op("dve", lambda e, o=ms, i=cv[:, j:j + 512], w=g.convw_sb[:, 4 * ct + j:4 * ct + j + 1]:
                       e.scalar_tensor_tensor(o, i, w[:, 0:1], o, ALU.mult, ALU.add),
                       reads=[rcv, rm, "convw"], writes=[rm], dbg=f"cnv{j}.{ti}.{ct}")
                    if j == 1:
                        op("pool", lambda e, o=tt_, i=cv[:, 2:2 + 512],
                           w=g.convw_sb[:, 4 * ct + 2:4 * ct + 3]:
                           e.tensor_tensor(o, i, w.broadcast_to([128, 512]), ALU.mult),
                           reads=[rcv, "convw"], writes=[rt], dbg=f"cnv2a.{ti}.{ct}")
                        op("pool", lambda e, o=ms, a=tt_:
                           e.tensor_tensor(o, o, a, ALU.add),
                           reads=[rt, rm], writes=[rm], dbg=f"cnv2b.{ti}.{ct}")
                src_ap = ms
                rsrc = rm
            else:
                src_ap = pq[:, 0:512]
                rsrc = rps
            op("act", lambda e, o=tt_, i=src_ap: e.activation(o, i, AF.Tanh, bias=0.0, scale=0.5),
               reads=[rsrc], writes=[rt], dbg=f"th{ti}.{ct}", aset="T")
            if ct < 16:
                op("pool", lambda e, o=tt_, i=src_ap: e.tensor_tensor(o, i, o, ALU.mult),
                   reads=[rsrc, rt], writes=[rt])
                op("dve", lambda e, o=ms, i=src_ap, t2=tt_: e.tensor_tensor(o, i, t2, ALU.add),
                   reads=[rsrc, rt], writes=[rm], dbg=f"sl{ti}.{ct}")
                outbuf, rout = ms, rm
            else:
                op("act", lambda e, o=ms, i=src_ap: e.activation(o, i, AF.Identity),
                   reads=[rsrc], writes=[rm], dbg=f"zcp{ti}.{ct}")
                op("pool", lambda e, o=tt_, i=ms: e.tensor_tensor(o, i, o, ALU.mult),
                   reads=[rm, rt], writes=[rt])
                op("dve", lambda e, o=ms, i=ms, t2=tt_: e.tensor_tensor(o, i, t2, ALU.add),
                   reads=[rm, rt], writes=[rm], dbg=f"zsl{ti}.{ct}")
                outbuf, rout = ms, rm
            if ct < 8:
                isq = ct < 4
                hk = ct % 4
                stripo = o_qts[par] if isq else o_kts[par]
                srg = ("qts" if isq else "kts") + str(par)
                ssqt = g.ssqQ if isq else g.ssqK
                ssrg = ("ssqq" if isq else "ssqk") + str(par)
                for cc in range(CPT):
                    op("pe", lambda e, o=psb[0][:, 128 * cc:128 * cc + 128],
                       i=outbuf[:, 128 * cc:128 * cc + 128], idn=g.iden128[:, :]:
                       e.transpose(o, i, idn),
                       reads=[rout, "iden128"], writes=["psb0"], dbg=f"Tq{ti}.{ct}.{cc}")
                dst = ab(stripo, 2048).rearrange("p (c h e) -> p c h e", c=4, h=4)[:, :, hk, :]
                op("act" if ct % 2 == 0 else "dve",
                   lambda e, d=dst, p=psb[0][:, 0:512].rearrange("p (c e) -> p c e", c=4):
                   (e.activation(d, p, AF.Identity) if hasattr(e, "activation") else e.tensor_copy(d, p)),
                   reads=["psb0"], writes=[srg], dbg=f"Tqe{ti}.{ct}")
                for cc in range(CPT):
                    sl = ab(stripo, 2048)[:, (cc * 4 + hk) * 128:(cc * 4 + hk) * 128 + 128]
                    op("dve", lambda e, s=sl,
                       t=ssqt[:, 16 * par + cc * 4 + hk:16 * par + cc * 4 + hk + 1],
                       jk=ab(o_jk2, 128):
                       e.scalar_tensor_tensor(jk, s, 1.0, s, ALU.mult, ALU.mult, accum_out=t),
                       reads=[srg], writes=["jk2", ssrg], dbg=f"ssq{ti}.{ct}.{cc}")
            else:
                hv = (ct - 8) % 8
                isv = ct < 16
                dsto = o_vts if isv else o_zts
                pfx = "vts" if isv else "zts"
                for cc in range(CPT):
                    op("pe", lambda e, o=psb[1][:, 128 * cc:128 * cc + 128],
                       i=outbuf[:, 128 * cc:128 * cc + 128], idn=g.iden128[:, :]:
                       e.transpose(o, i, idn),
                       reads=[rout, "iden128"], writes=["psb1"], dbg=f"Tv{ti}.{ct}.{cc}")
                for cc2 in range(2):
                    dst = ab(dsto, 4096).rearrange("p (c h e) -> p c h e", c=4, h=8)[
                        :, 2 * cc2:2 * cc2 + 2, hv, :]
                    op("act" if ct % 2 == 0 else "dve",
                       lambda e, d=dst, p=psb[1][:, 256 * cc2:256 * cc2 + 256].rearrange(
                           "p (c e) -> p c e", c=2):
                       (e.activation(d, p, AF.Identity) if hasattr(e, "activation") else e.tensor_copy(d, p)),
                       reads=["psb1"], writes=[f"{pfx}{2 * cc2}", f"{pfx}{2 * cc2 + 1}"],
                       dbg=f"Tve{ti}.{ct}.{cc2}")
            yield
        if part == 0:
            return
        op("act", lambda e: e.activation(g.rsqQ[:, 16 * par:16 * par + 16],
                                         g.ssqQ[:, 16 * par:16 * par + 16], AF.Ln, bias=g.cstc[:, 4:5]),
           reads=[f"ssqq{par}", "cstc"], writes=[f"rsqq{par}"], dbg=f"lnq{ti}", aset="L")
        op("act", lambda e: e.activation(g.rsqQ[:, 16 * par:16 * par + 16],
                                         g.rsqQ[:, 16 * par:16 * par + 16], AF.Exp, bias=0.0, scale=-0.5),
           reads=[f"rsqq{par}"], writes=[f"rsqq{par}"])
        op("act", lambda e: e.activation(g.rsqK[:, 16 * par:16 * par + 16],
                                         g.ssqK[:, 16 * par:16 * par + 16], AF.Ln, bias=g.cstc[:, 4:5]),
           reads=[f"ssqk{par}", "cstc"], writes=[f"rsqk{par}"], dbg=f"lnk{ti}", aset="L")
        op("act", lambda e: e.activation(g.rsqK[:, 16 * par:16 * par + 16],
                                         g.rsqK[:, 16 * par:16 * par + 16], AF.Exp, bias=0.0, scale=-0.5),
           reads=[f"rsqk{par}"], writes=[f"rsqk{par}"])
        yield

    def finish_xg(tp):
        rst = g.rstd[tp % 2]
        rr = f"rstd{tp % 2}"
        rmq = f"msq{tp % 2}"
        rxg = "xg"
        op("act", lambda e, m=g.msq[tp % 2]: e.activation(rst[:, :], m[:, :], AF.Ln,
                                                          bias=g.cstc[:, 3:4], scale=1.0 / 128.0),
           reads=[rmq, "cstc"], writes=[rr], dbg=f"lnr{tp}", aset="L")
        op("act", lambda e: e.activation(rst[:, :], rst[:, :], AF.Exp, bias=0.0, scale=-0.5),
           reads=[rr], writes=[rr])
        xb_ = ab(o_xg, 4096)
        for cc2 in range(CPT):
            for hv in range(HVC):
                sl = xb_[:, 1024 * cc2 + 128 * hv:1024 * cc2 + 128 * hv + 128]
                rc = rst[:, 8 * cc2 + hv:8 * cc2 + hv + 1]
                if hv % 2:
                    op("pool", lambda e, s=sl, r=rc:
                       e.tensor_tensor(s, s, r.broadcast_to([128, 128]), ALU.mult),
                       reads=[rr, rxg], writes=[rxg], dbg=f"xgn{tp}.{cc2}.{hv}")
                else:
                    op("dve", lambda e, s=sl, r=rc:
                       e.tensor_scalar(s, s, r[:, 0:1], None, ALU.mult),
                       reads=[rr, rxg], writes=[rxg], dbg=f"xgn{tp}.{cc2}.{hv}")
        dma(nch("cs"), lambda e, t_=tp, x2=xb_: e.dma_start(
            out=g.xg_con[512 * t_:512 * t_ + 512, :].rearrange("(c s) n -> s c n", c=4),
            in_=x2.rearrange("p (c n) -> p c n", c=4)),
            reads=[rxg], writes=["xg_con"], dbg=f"stxg{tp}")

    def gen_body(ti):
        par0 = ti % 2
        for cc in range(CPT):
            c = CPT * ti + cc
            par = cc % 2
            for pfx, stripo, rsq, rrg, hoff, pboff in (
                    ("q", o_qts[par0], g.rsqQ, f"rsqq{par0}", o_qh, 0),
                    ("k", o_kts[par0], g.rsqK, f"rsqk{par0}", o_kh, 512)):
                strip_rg = pfx + f"ts{par0}"
                for hk in range(4):
                    sl = ab(stripo, 2048)[:, (cc * 4 + hk) * 128:(cc * 4 + hk) * 128 + 128]
                    rcol = rsq[:, 16 * par0 + cc * 4 + hk:16 * par0 + cc * 4 + hk + 1]
                    if hk % 2:
                        op("pool", lambda e, s=sl, r=rcol:
                           e.tensor_tensor(s, s, r.broadcast_to([128, 128]), ALU.mult),
                           reads=[rrg, strip_rg], writes=[strip_rg], dbg=f"nrm{pfx}{ti}.{cc}.{hk}")
                    else:
                        op("dve", lambda e, s=sl, r=rcol:
                           e.tensor_scalar(s, s, r[:, 0:1], None, ALU.mult),
                           reads=[rrg, strip_rg], writes=[strip_rg], dbg=f"nrm{pfx}{ti}.{cc}.{hk}")
                    op("pe", lambda e, o=psb[0][:, pboff + 128 * hk:pboff + 128 * hk + 128],
                       i=sl, idn=g.iden128[:, :]: e.transpose(o, i, idn),
                       reads=[strip_rg, "iden128"], writes=["psb0"], dbg=f"bT{pfx}{ti}.{cc}.{hk}")
                op("act", lambda e, d=ab(hoff[par], 512), p=psb[0][:, pboff:pboff + 512]:
                   e.activation(d, p, AF.Identity),
                   reads=["psb0"], writes=[pfx + f"hat{par}"], dbg=f"bTe{pfx}{ti}.{cc}")
            if g.dbg_qts is not None and ti == 0 and cc == 3:
                dma(nch("cs"), lambda e: e.dma_start(out=g.dbg_qts, in_=ab(o_qts[0], 2048)),
                    reads=["qts0"], writes=["dbgq"], dbg="dbgq")
                dma(nch("cs"), lambda e: e.dma_start(out=g.dbg_kts, in_=ab(o_kts[0], 2048)),
                    reads=["kts0"], writes=["dbgk"], dbg="dbgk")
            yield
            for hv in range(HVC):
                tid, gb, sl_ = _rowslot(hv)
                bank = 2 if hv < 4 else 3
                slot = psf[bank][:, 128 * (hv % 4):128 * (hv % 4) + 128]
                op("pe", lambda e, s=slot, l=g.selb[gb:gb + 4, 128 * sl_:128 * sl_ + 128],
                   r=g.rows2[tid][gb:gb + 4, 128 * c:128 * c + 128]:
                   e.matmul(s, l, r, start=True, stop=False),
                   reads=["selb", "rows2"], writes=[f"psf{bank}"], dbg=f"D1.{ti}.{cc}.{hv}")
                op("pe", lambda e, s=slot: e.matmul(s, g.iden128[:, :], g.mask128[:, :],
                                                    start=False, stop=True),
                   reads=["iden128", "mask128"], writes=[f"psf{bank}"], dbg=f"D3.{ti}.{cc}.{hv}")
                op("act", lambda e, s=slot, o=af(o_E[par], 1024)[:, 128 * hv:128 * hv + 128],
                   b=g.negG_sl[:, 8 * c + hv:8 * c + hv + 1]:
                   e.activation(o, s, AF.Exp, bias=b[:, 0:1], scale=1.0),
                   reads=[f"psf{bank}", "negG_sl"], writes=[f"E{par}"], dbg=f"E.{ti}.{cc}.{hv}")
            if g.dbg_E is not None and ti == 0:
                dma(nch("cs"), lambda e, c2=cc, p2=par: e.dma_start(
                    out=g.dbg_E[:, 1024 * c2:1024 * c2 + 1024], in_=af(o_E[p2], 1024)),
                    reads=[f"E{par}"], writes=["dbgE"], dbg=f"dbgE{cc}")
            for hk in range(4):
                bank = 4 if hk < 2 else 5
                pk = psf[bank][:, 256 * (hk % 2):256 * (hk % 2) + 256]
                for half, srcoff in ((0, o_qh[par]), (1, o_kh[par])):
                    op("pe", lambda e, p=pk[:, 128 * half:128 * half + 128],
                       l=ab(o_kh[par], 512)[:, 128 * hk:128 * hk + 128],
                       r=ab(srcoff, 512)[:, 128 * hk:128 * hk + 128]:
                       e.matmul(p, l, r, start=True, stop=True),
                       reads=[f"khat{par}", f"qhat{par}"], writes=[f"psf{bank}"],
                       dbg=f"P1.{ti}.{cc}.{hk}.{half}")
            yield
            for hv in range(HVC):
                hk = hv // 2
                pbank = 4 if hk < 2 else 5
                pk = psf[pbank][:, 256 * (hk % 2):256 * (hk % 2) + 256]
                pet = ab(o_PE, 2048)[:, 256 * hv:256 * hv + 256]
                ev = af(o_E[par], 1024)[:, 128 * hv:128 * hv + 128]
                op("dve", lambda e, o=pet, p=pk, e2=ev:
                   e.tensor_tensor(o.rearrange("s (r n) -> s r n", r=2),
                                   p.rearrange("s (r n) -> s r n", r=2),
                                   e2.unsqueeze(1).broadcast_to([128, 2, 128]), ALU.mult),
                   reads=[f"psf{pbank}", f"E{par}"], writes=["PEX"], dbg=f"PEt.{ti}.{cc}.{hv}")
                op("pool", lambda e, o=ab(o_A[par], 1024)[:, 128 * hv:128 * hv + 128],
                   i=pet[:, 0:128], b=g.betaS_sl[:, 8 * c + hv:8 * c + hv + 1]:
                   e.tensor_tensor(o, i, b.broadcast_to([128, 128]), ALU.mult),
                   reads=["PEX", "betaS_sl"], writes=[f"A{par}"], dbg=f"A.{ti}.{cc}.{hv}")
                op("dve", lambda e, o=ab(o_NT[0], 1024)[:, 128 * hv:128 * hv + 128],
                   i=pet[:, 128:256], b=g.beta_tok[:, 8 * c + hv:8 * c + hv + 1]:
                   e.scalar_tensor_tensor(o, i, b[:, 0:1], g.strict128[:, :], ALU.mult, ALU.mult),
                   reads=["PEX", "beta", "strict128"], writes=["NT_0"], dbg=f"NT0.{ti}.{cc}.{hv}")
            yield
            for hv in range(HVC):
                op("pe", lambda e, o=psb[1][:, 128 * hv:128 * hv + 128],
                   i=ab(o_NT[0], 1024)[:, 128 * hv:128 * hv + 128],
                   idn=g.iden128[:, :]: e.transpose(o, i, idn),
                   reads=["NT_0", "iden128"], writes=["psb1"], dbg=f"N0T.{ti}.{cc}.{hv}")
            op("dve", lambda e, d=ab(o_N[0], 1024), p=psb[1][:, :]: e.tensor_copy(d, p),
               reads=["psb1"], writes=["Nb0"], dbg=f"N0e.{ti}.{cc}")

            def sq_level(k):
                ntk, ntkr = nt_buf(k)
                ntk1, ntk1r = nt_buf(k + 1)
                for hv in range(HVC):
                    bank = 2 if hv < 4 else 3
                    op("pe", lambda e, p=psf[bank][:, 128 * (hv % 4):128 * (hv % 4) + 128],
                       l=ab(o_N[k % 2], 1024)[:, 128 * hv:128 * hv + 128],
                       r=ntk[:, 128 * hv:128 * hv + 128]:
                       e.matmul(p, l, r, start=True, stop=True),
                       reads=[f"Nb{k % 2}", ntkr], writes=[f"psf{bank}"], dbg=f"sqT{k}.{ti}.{cc}.{hv}")
                for b_ in (2, 3):
                    op("act" if k % 2 else "dve",
                       lambda e, d=ntk1[:, 512 * (b_ - 2):512 * (b_ - 2) + 512], p=psf[b_][:, :]:
                       (e.activation(d, p, AF.Identity) if hasattr(e, "activation") else e.tensor_copy(d, p)),
                       reads=[f"psf{b_}"], writes=[ntk1r], dbg=f"sqTe{k}.{b_}.{ti}.{cc}")
                if k < 5:
                    for hv in range(HVC):
                        bank = 4 if hv < 4 else 5
                        op("pe", lambda e, p=psf[bank][:, 128 * (hv % 4):128 * (hv % 4) + 128],
                           l=ntk[:, 128 * hv:128 * hv + 128],
                           r=ab(o_N[k % 2], 1024)[:, 128 * hv:128 * hv + 128]:
                           e.matmul(p, l, r, start=True, stop=True),
                           reads=[f"Nb{k % 2}", ntkr], writes=[f"psf{bank}"], dbg=f"sqN{k}.{ti}.{cc}.{hv}")
                    for b_ in (4, 5):
                        op("dve" if k % 2 else "act",
                           lambda e, d=ab(o_N[(k + 1) % 2], 1024)[:, 512 * (0 if b_ == 4 else 1):
                                                                  512 * (0 if b_ == 4 else 1) + 512],
                           p=psf[b_][:, :]:
                           (e.activation(d, p, AF.Identity) if hasattr(e, "activation") else e.tensor_copy(d, p)),
                           reads=[f"psf{b_}"], writes=[f"Nb{(k + 1) % 2}"], dbg=f"sqNe{k}.{b_}.{ti}.{cc}")
            for k in range(3):
                sq_level(k)
            yield
            for k in range(3, 6):
                sq_level(k)
            for hv in range(HVC):
                hk = hv // 2
                kc = af(o_kc, 32)[:, 8 * par + hv:8 * par + hv + 1]
                op("dve", lambda e, o=kc, b=g.beta_tok[:, 8 * c + hv:8 * c + hv + 1],
                   ec=af(o_E[par], 1024)[:, 128 * hv + 127:128 * hv + 128]:
                   e.tensor_tensor(o, b, ec, ALU.mult),
                   reads=["beta", f"E{par}"], writes=[f"kdc{par}"], dbg=f"kdc.{ti}.{cc}.{hv}")
                op("pool", lambda e, o=ab(o_kd[par], 1024)[:, 128 * hv:128 * hv + 128],
                   i=ab(o_kts[par0], 2048)[:, (cc * 4 + hk) * 128:(cc * 4 + hk) * 128 + 128],
                   s=kc: e.tensor_tensor(o, i, s.broadcast_to([128, 128]), ALU.mult),
                   reads=[f"kts{par0}", f"kdc{par}"], writes=[f"kd{par}"], dbg=f"kd.{ti}.{cc}.{hv}")
            yield
            for hv in range(HVC):
                hk = hv // 2
                bank = 4 if hv < 4 else 5
                op("pe", lambda e, p=psf[bank][:, 128 * (hv % 4):128 * (hv % 4) + 128],
                   l=ab(o_kh[par], 512)[:, 128 * hk:128 * hk + 128],
                   r=g.S_bf[:, 128 * hv:128 * hv + 128]:
                   e.matmul(p, l, r, start=True, stop=True),
                   reads=[f"khat{par}", f"Sb{hv}"], writes=[f"psf{bank}"], dbg=f"KS.{ti}.{cc}.{hv}")
            for hv in range(HVC):
                bank = 4 if hv < 4 else 5
                op("dve", lambda e, o=ab(o_yb[0], 1024)[:, 128 * hv:128 * hv + 128],
                   i=psf[bank][:, 128 * (hv % 4):128 * (hv % 4) + 128],
                   s=g.neG_sl[:, 8 * c + hv:8 * c + hv + 1],
                   v=ab(o_vts, 4096)[:, 1024 * cc + 128 * hv:1024 * cc + 128 * hv + 128]:
                   e.scalar_tensor_tensor(o, i, s[:, 0:1], v, ALU.mult, ALU.add),
                   reads=[f"psf{bank}", "neG_sl", f"vts{cc}"], writes=["yb0"], dbg=f"r.{ti}.{cc}.{hv}")
            if g.dbg_r is not None and ti == 0:
                dma(nch("cs"), lambda e, c2=cc: e.dma_start(
                    out=g.dbg_r[:, 1024 * c2:1024 * c2 + 1024], in_=ab(o_yb[0], 1024)),
                    reads=["yb0"], writes=["dbgr"], dbg=f"dbgr{cc}")
            yield
            for half8 in range(2):
                for l_ in ((0, 1, 2, 3) if half8 == 0 else (4, 5, 6)):
                    kk = 6 - l_
                    ntk, ntkr = nt_buf(kk)
                    use_act = (l_ % 2 == 1)
                    for hv in range(HVC):
                        bank = 2 if hv < 4 else 3
                        p_ = psf[bank][:, 128 * (hv % 4):128 * (hv % 4) + 128]
                        if use_act:
                            op("pe", lambda e, p=p_,
                               r=ab(o_yb[l_ % 2], 1024)[:, 128 * hv:128 * hv + 128]:
                               e.matmul(p, g.iden128[:, :], r, start=True, stop=False),
                               reads=["iden128", f"yb{l_ % 2}"], writes=[f"psf{bank}"],
                               dbg=f"apI{l_}.{ti}.{cc}.{hv}")
                        op("pe", lambda e, p=p_, l=ntk[:, 128 * hv:128 * hv + 128],
                           r=ab(o_yb[l_ % 2], 1024)[:, 128 * hv:128 * hv + 128],
                           s=(not use_act): e.matmul(p, l, r, start=s, stop=True),
                           reads=[ntkr, f"yb{l_ % 2}"], writes=[f"psf{bank}"],
                           dbg=f"ap{l_}.{ti}.{cc}.{hv}")
                    for b_ in (2, 3):
                        if use_act:
                            op("act", lambda e, o=ab(o_yb[(l_ + 1) % 2], 1024)[
                                :, 512 * (b_ - 2):512 * (b_ - 2) + 512], p=psf[b_][:, :]:
                               e.activation(o, p, AF.Identity),
                               reads=[f"psf{b_}"], writes=[f"yb{(l_ + 1) % 2}"],
                               dbg=f"apeA{l_}.{b_}.{ti}.{cc}")
                        else:
                            op("dve", lambda e, o=ab(o_yb[(l_ + 1) % 2], 1024)[
                                :, 512 * (b_ - 2):512 * (b_ - 2) + 512],
                               a=ab(o_yb[l_ % 2], 1024)[:, 512 * (b_ - 2):512 * (b_ - 2) + 512],
                               p=psf[b_][:, :]: e.tensor_tensor(o, a, p, ALU.add),
                               reads=[f"psf{b_}", f"yb{l_ % 2}"], writes=[f"yb{(l_ + 1) % 2}"],
                               dbg=f"ape{l_}.{b_}.{ti}.{cc}")
                yield
            for hv in range(HVC):
                bank = 2 if hv < 4 else 3
                op("pe", lambda e, p=psf[bank][:, 128 * (hv % 4):128 * (hv % 4) + 128],
                   l=ab(o_A[par], 1024)[:, 128 * hv:128 * hv + 128],
                   r=ab(o_yb[1], 1024)[:, 128 * hv:128 * hv + 128]:
                   e.matmul(p, l, r, start=True, stop=True),
                   reads=[f"A{par}", "yb1"], writes=[f"psf{bank}"], dbg=f"Au.{ti}.{cc}.{hv}")
            for hv in range(HVC):
                hk = hv // 2
                bank = 4 if hv < 4 else 5
                op("pe", lambda e, p=psf[bank][:, 128 * (hv % 4):128 * (hv % 4) + 128],
                   l=ab(o_qh[par], 512)[:, 128 * hk:128 * hk + 128],
                   r=g.S_bf[:, 128 * hv:128 * hv + 128]:
                   e.matmul(p, l, r, start=True, stop=True),
                   reads=[f"qhat{par}", f"Sb{hv}"], writes=[f"psf{bank}"], dbg=f"QS.{ti}.{cc}.{hv}")
            for hv in range(HVC):
                bank = 4 if hv < 4 else 5
                op("act", lambda e, o=af(o_oq, 1024)[:, 128 * hv:128 * hv + 128],
                   i=psf[bank][:, 128 * (hv % 4):128 * (hv % 4) + 128],
                   s=g.eG8_sl[:, 8 * c + hv:8 * c + hv + 1]:
                   e.activation(o, i, AF.Identity, bias=0.0, scale=s[:, 0:1]),
                   reads=[f"psf{bank}", "eG8_sl"], writes=["oq"], dbg=f"oq.{ti}.{cc}.{hv}")
            for b_ in (2, 3):
                op("dve", lambda e, o=af(o_ob, 1024)[:, 512 * (b_ - 2):512 * (b_ - 2) + 512],
                   a=af(o_oq, 1024)[:, 512 * (b_ - 2):512 * (b_ - 2) + 512],
                   p=psf[b_][:, :]: e.tensor_tensor(o, a, p, ALU.add),
                   reads=[f"psf{b_}", "oq"], writes=["ob"], dbg=f"o.{b_}.{ti}.{cc}")
            if g.dbg_ob is not None and ti == 0:
                dma(nch("cs"), lambda e, c2=cc: e.dma_start(
                    out=g.dbg_ob[:, 1024 * c2:1024 * c2 + 1024], in_=af(o_ob, 1024)),
                    reads=["ob"], writes=["dbgo"], dbg=f"dbgo{cc}")
                dma(nch("cs"), lambda e, c2=cc: e.dma_start(
                    out=g.dbg_u[:, 1024 * c2:1024 * c2 + 1024], in_=ab(o_yb[1], 1024)),
                    reads=["yb1"], writes=["dbgu"], dbg=f"dbgu{cc}")
            for hv in range(HVC):
                bank = 2 if hv < 4 else 3
                op("pe", lambda e, p=psf[bank][:, 128 * (hv % 4):128 * (hv % 4) + 128],
                   l=ab(o_kd[par], 1024)[:, 128 * hv:128 * hv + 128],
                   r=ab(o_yb[1], 1024)[:, 128 * hv:128 * hv + 128]:
                   e.matmul(p, l, r, start=True, stop=True),
                   reads=[f"kd{par}", "yb1"], writes=[f"psf{bank}"], dbg=f"SU.{ti}.{cc}.{hv}")
            for hv in range(HVC):
                bank = 2 if hv < 4 else 3
                op("dve", lambda e, o=g.S_sb[:, 128 * hv:128 * hv + 128],
                   s=g.egC_all[:, 8 * c + hv:8 * c + hv + 1],
                   p=psf[bank][:, 128 * (hv % 4):128 * (hv % 4) + 128]:
                   e.scalar_tensor_tensor(o, o, s[:, 0:1], p, ALU.mult, ALU.add),
                   reads=[f"Ss{hv}", "egC", f"psf{bank}"], writes=[f"Ss{hv}"],
                   dbg=f"Sup.{ti}.{cc}.{hv}")
                op("pool", lambda e, o=g.S_bf[:, 128 * hv:128 * hv + 128],
                   i=g.S_sb[:, 128 * hv:128 * hv + 128]: e.tensor_copy(o, i),
                   reads=[f"Ss{hv}"], writes=[f"Sb{hv}"], dbg=f"Sbf.{ti}.{cc}.{hv}")
            if g.dbg_S is not None and ti == 0 and cc == 0:
                dma(nch("cs"), lambda e: e.dma_start(out=g.dbg_vts, in_=ab(o_vts, 4096)),
                    reads=[f"vts{c2}" for c2 in range(4)], writes=["dbgv"], dbg="dbgv")
                dma(nch("cs"), lambda e: e.dma_start(out=g.dbg_zts, in_=ab(o_zts, 4096)),
                    reads=[f"zts{c2}" for c2 in range(4)], writes=["dbgz"], dbg="dbgz")
                dma(nch("cs"), lambda e: e.dma_start(out=g.dbg_S, in_=g.S_bf[:, :]),
                    reads=[f"Sb{hv2}" for hv2 in range(8)], writes=["dbgS"], dbg="dbgS")
                dma(nch("cs"), lambda e: e.dma_start(out=g.dbg_Ss, in_=g.S_sb[:, :]),
                    reads=[f"Ss{hv2}" for hv2 in range(8)], writes=["dbgSs"], dbg="dbgSs")
                dma(nch("cs"), lambda e: e.dma_start(out=g.dbg_kd, in_=ab(o_kd[0], 1024)),
                    reads=["kd0"], writes=["dbgkd"], dbg="dbgkd")
            op("pool", lambda e, o=ab(o_xg, 4096)[:, 1024 * cc:1024 * cc + 1024],
               a=af(o_ob, 1024), z=ab(o_zts, 4096)[:, 1024 * cc:1024 * cc + 1024]:
               e.tensor_tensor(o, a, z, ALU.mult),
               reads=["ob", f"zts{cc}"], writes=["xg"], dbg=f"xg0.{ti}.{cc}")
            op("act", lambda e, o=ab(o_jk, 1024), i=ab(o_xg, 4096)[:, 1024 * cc:1024 * cc + 1024]:
               e.activation(o, i, AF.Square),
               reads=["xg"], writes=["jk"], dbg=f"sq.{ti}.{cc}")
            op("dve", lambda e, o=g.msq[ti % 2][:, 8 * cc:8 * cc + 8].unsqueeze(2),
               i=ab(o_jk, 1024).rearrange("p (j e) -> p j e", j=8):
               e.tensor_reduce(o, i, AX.X, ALU.add),
               reads=["jk"], writes=[f"msq{ti % 2}"], dbg=f"msq.{ti}.{cc}")
            yield
        finish_xg(ti)
        yield

    def drain(gen):
        for _ in gen:
            pass

    heads_qk = [gen_head(t, 0) for t in range(NTT)]
    heads_vz = [gen_head(t, 1) for t in range(NTT)]
    bodies = [gen_body(t) for t in range(NTT)]
    drain(heads_qk[0])
    drain(heads_vz[0])
    for ti in range(NTT):
        h = heads_qk[ti + 1] if ti + 1 < NTT else None
        b = bodies[ti]
        while True:
            done_b = next(b, StopIteration) is StopIteration
            if h is not None:
                next(h, None)
                if done_b:
                    drain(h)
            if done_b:
                break
        if ti + 1 < NTT:
            drain(heads_vz[ti + 1])
    dma(g.cc2, lambda e: e.collective_compute(
        "AllGather", ALU.bypass,
        replica_groups=[[0, 1, 2, 3], [4, 5, 6, 7]],
        ins=[g.xg_con.opt()], outs=[g.xg_all.opt()]),
        reads=["xg_con"], writes=["xg_all"], dbg="AG2", cost=90000)

    sched.barrier(keep=("xg_all",), exclude=(g.cc2,))
    cab = _Carve(AB.shape[1])
    caf = _Carve(AFW.shape[1])
    o_xnl2 = cab.take(16 * TSHARD)
    o_wgt = [cab.take(2048) for _ in range(4)]
    o_wdh = [cab.take(11264), cab.take(11264)]
    o_wdf = o_wdh[0]
    o_h = cab.take(11264)
    o_ft = [cab.take(512), cab.take(512)]
    o_fb = [cab.take(512), cab.take(512)]
    o_ys = [caf.take(2048), caf.take(2048)]

    xnl2 = AB[:, o_xnl2:o_xnl2 + 16 * TSHARD]
    dma(nch("cs"), lambda e: e.dma_start(out=xnl2.rearrange("p (c t) -> p c t", c=16),
                                         in_=g.xn_con.rearrange("(c p) t -> p c t", p=128)),
        reads=["xn_con"], writes=["xnl2"], dbg="ldxnl2", cost=12000)

    _guctr = [0]

    def gu_evac(pg, pu, rgps, rups, hdst, rh, tag):
        _guctr[0] += 1
        par2 = _guctr[0] % 2
        ft = ab(o_ft[par2], 512)
        fb = ab(o_fb[par2], 512)
        rft, rfb = f"ft{par2}", f"fb{par2}"
        op("act", lambda e: e.activation(ft, pg, AF.Tanh, bias=0.0, scale=0.5),
           reads=[rgps], writes=[rft], dbg="t" + tag)
        op("dve", lambda e: e.scalar_tensor_tensor(ft, pg, 0.5, ft, ALU.mult, ALU.mult),
           reads=[rgps, rft], writes=[rft])
        op("dve", lambda e: e.scalar_tensor_tensor(fb, pg, 0.5, ft, ALU.mult, ALU.add),
           reads=[rgps, rft], writes=[rfb])
        op("dve", lambda e: e.tensor_tensor(hdst, fb, pu, ALU.mult),
           reads=[rfb, rups], writes=[rh], dbg="h" + tag)

    def ffn_expert(ei):
        for ht in range(11):
            wgt = ab(o_wgt[ht % 2], 2048)
            wut = ab(o_wgt[2 + ht % 2], 2048)
            rg_, ru_ = f"wgt{ht % 2}", f"wut{ht % 2}"
            dma(nch("cw"), lambda e, d=wgt.rearrange("p (c n) -> p c n", c=16), h_=ht: e.dma_start(
                out=d, in_=g.wg_d[ei, :, 128 * h_:128 * h_ + 128].rearrange("(c p) n -> p c n", p=128)),
                reads=[], writes=[rg_], dbg=f"lwg{ei}.{ht}", cost=3000)
            dma(nch("cw"), lambda e, d=wut.rearrange("p (c n) -> p c n", c=16), h_=ht: e.dma_start(
                out=d, in_=g.wu_d[ei, :, 128 * h_:128 * h_ + 128].rearrange("(c p) n -> p c n", p=128)),
                reads=[], writes=[ru_], dbg=f"lwu{ei}.{ht}", cost=3000)
            for j2 in range(2):
                bk = 2 * (ht % 2)
                for which, wt, rw_, pix in (("g", wgt, rg_, bk), ("u", wut, ru_, bk + 1)):
                    for d16 in range(16):
                        op("pe", lambda e, p=psf[pix][:, 0:512], l=wt[:, 128 * d16:128 * d16 + 128],
                           r=xnl2.rearrange("p (c t) -> p c t", c=16)[:, d16, 512 * j2:512 * j2 + 512],
                           s=(d16 == 0), t=(d16 == 15): e.matmul(p, l, r, start=s, stop=t),
                           reads=[rw_, "xnl2"], writes=[f"psf{pix}"],
                           dbg=f"gu{ei}.{ht}.{j2}.{which}.{d16}", cost=230)
                gu_evac(psf[bk][:, 0:512], psf[bk + 1][:, 0:512], f"psf{bk}", f"psf{bk + 1}",
                        AB[:, o_h + 1024 * ht + 512 * j2:o_h + 1024 * ht + 512 * j2 + 512],
                        f"h{ht}", f"{ei}.{ht}.{j2}")
        for dh in range(2):
            for sp_ in range(2):
                a0, a1 = (0, 6) if sp_ == 0 else (6, 11)
                dma(nch("cw"), lambda e, dh_=dh, a0_=a0, a1_=a1: e.dma_start(
                    out=AB[:, o_wdh[dh_]:o_wdh[dh_] + 11264].rearrange("p (a n) -> p a n", a=11)[:, a0_:a1_, :],
                    in_=g.wd_d[ei].rearrange("(a p) n -> p a n", p=128)[:, a0_:a1_, 1024 * dh_:1024 * dh_ + 1024]),
                    reads=[], writes=[f"wd{dh}s{sp_}"], dbg=f"lwd{ei}.{dh}.{sp_}", cost=8000)
        for dh in range(2):
            for ts_ in range(8):
                yi = 8 * dh + ts_
                ys = af(o_ys[yi % 2], 2048)[:, 0:1024]
                rys = f"ys{yi % 2}"
                for dt in range(2):
                    pd = psf[4 + dt % 2]
                    rpd = f"psf{4 + dt % 2}"
                    for ht in range(11):
                        op("pe", lambda e, p=pd[:, 0:512],
                           l=AB[:, o_h + 1024 * ht + 128 * ts_:o_h + 1024 * ht + 128 * ts_ + 128],
                           r=AB[:, o_wdh[dh] + 1024 * ht + 512 * dt:o_wdh[dh] + 1024 * ht + 512 * dt + 512],
                           s=(ht == 0), t=(ht == 10): e.matmul(p, l, r, start=s, stop=t),
                           reads=[f"h{ht}", f"wd{dh}s{0 if ht < 6 else 1}"], writes=[rpd],
                           dbg=f"dn{ei}.{dh}.{ts_}.{dt}.{ht}", cost=230)
                    op("dve", lambda e, o=ys[:, 512 * dt:512 * dt + 512], p=pd[:, 0:512],
                       s=g.gates_sb[:, 8 * ts_ + ei:8 * ts_ + ei + 1]:
                       e.tensor_scalar(o, p, s[:, 0:1], None, ALU.mult),
                       reads=[rpd, "gates"], writes=[rys], dbg=f"dne{ei}.{dh}.{ts_}.{dt}")
                dma(nch("cy"), lambda e, t_=ts_, dh_=dh, y2=ys, acc=(ei > 0): e.dma_start(
                    out=g.y_dram[128 * t_:128 * t_ + 128, 1024 * dh_:1024 * dh_ + 1024], in_=y2,
                    accum_op=(ALU.add if acc else ALU.bypass)),
                    reads=[rys], writes=[f"y{ts_}.{dh}"], dbg=f"ydma{ei}.{dh}.{ts_}", cost=1500)

    for ei in range(E_EXP):
        ffn_expert(ei)

    sched.barrier()
    h_sh = nc.dram_tensor("h_sh", [HSH, TSHARD], BF16).ap()
    for ht in range(22):
        wgt = ab(o_wgt[ht % 2], 2048)
        wut = ab(o_wgt[2 + ht % 2], 2048)
        rg_, ru_ = f"wgt{ht % 2}", f"wut{ht % 2}"
        dma(nch("cw"), lambda e, d=wgt.rearrange("p (c n) -> p c n", c=16), h_=ht: e.dma_start(
            out=d, in_=g.sg_d[:, 128 * h_:128 * h_ + 128].rearrange("(c p) n -> p c n", p=128)),
            reads=[], writes=[rg_], dbg=f"lsg{ht}", cost=3000)
        dma(nch("cw"), lambda e, d=wut.rearrange("p (c n) -> p c n", c=16), h_=ht: e.dma_start(
            out=d, in_=g.su_d[:, 128 * h_:128 * h_ + 128].rearrange("(c p) n -> p c n", p=128)),
            reads=[], writes=[ru_], dbg=f"lsu{ht}", cost=3000)
        for j2 in range(2):
            bk = 2 * (ht % 2)
            for which, wt, rw_, pix in (("g", wgt, rg_, bk), ("u", wut, ru_, bk + 1)):
                for d16 in range(16):
                    op("pe", lambda e, p=psf[pix][:, 0:512], l=wt[:, 128 * d16:128 * d16 + 128],
                       r=xnl2.rearrange("p (c t) -> p c t", c=16)[:, d16, 512 * j2:512 * j2 + 512],
                       s=(d16 == 0), t=(d16 == 15): e.matmul(p, l, r, start=s, stop=t),
                       reads=[rw_, "xnl2"], writes=[f"psf{pix}"],
                       dbg=f"sgu{ht}.{j2}.{which}.{d16}", cost=230)
            gu_evac(psf[bk][:, 0:512], psf[bk + 1][:, 0:512], f"psf{bk}", f"psf{bk + 1}",
                    AB[:, o_h + 512 * (j2):o_h + 512 * (j2) + 512], "hsh", f"s.{ht}.{j2}")
        dma(nch("cs"), lambda e, h_=ht: e.dma_start(out=h_sh[128 * h_:128 * h_ + 128, :],
                                                    in_=AB[:, o_h:o_h + 1024]),
            reads=["hsh"], writes=["h_sh"], dbg=f"sths{ht}")
    for half in range(2):
        for sp_ in range(2):
            a0, a1 = (0, 11) if sp_ == 0 else (11, 22)
            dma(nch("cw"), lambda e, hf=half, a0_=a0, a1_=a1: e.dma_start(
                out=AB[:, o_wdf:o_wdf + 22528].rearrange("p (a n) -> p a n", a=22)[:, a0_:a1_, :],
                in_=g.sd_d[:, 1024 * hf:1024 * hf + 1024].rearrange("(a p) n -> p a n", p=128)[:, a0_:a1_, :]),
                reads=[], writes=[f"wdfs{sp_}"], dbg=f"lsd{half}.{sp_}", cost=16000)
        for ts_ in range(8):
            htb = ab(o_h + 4096, 2816)
            rhtb = "htb"
            dma(nch("cs"), lambda e, t_=ts_, d=htb.rearrange("p (a t) -> p a t", a=22): e.dma_start(
                out=d, in_=h_sh[:, 128 * t_:128 * t_ + 128].rearrange("(a p) t -> p a t", p=128)),
                reads=["h_sh"], writes=[rhtb], dbg=f"ldhs{half}.{ts_}")
            ys = af(o_ys[ts_ % 2], 2048)[:, 0:1024]
            rys = f"ys{ts_ % 2}"
            for dt in range(2):
                pd = psf[4 + dt % 2]
                rpd = f"psf{4 + dt % 2}"
                for ht in range(22):
                    op("pe", lambda e, p=pd[:, 0:512],
                       l=htb[:, 128 * ht:128 * ht + 128],
                       r=AB[:, o_wdf + 1024 * ht + 512 * dt:o_wdf + 1024 * ht + 512 * dt + 512],
                       s=(ht == 0), t=(ht == 21): e.matmul(p, l, r, start=s, stop=t),
                       reads=[rhtb, f"wdfs{0 if ht < 11 else 1}"], writes=[rpd],
                       dbg=f"sdn{half}.{ts_}.{dt}.{ht}", cost=230)
                op("dve", lambda e, o=ys[:, 512 * dt:512 * dt + 512], p=pd[:, 0:512]:
                   e.tensor_copy(o, p),
                   reads=[rpd], writes=[rys], dbg=f"sdne{half}.{ts_}.{dt}")
            dma(nch("cy"), lambda e, t_=ts_, hf=half, y2=ys: e.dma_start(
                out=g.y_dram[128 * t_:128 * t_ + 128, 1024 * hf:1024 * hf + 1024], in_=y2,
                accum_op=ALU.add),
                reads=[rys], writes=[f"y{ts_}"], dbg=f"sydma{half}.{ts_}")

    sched.barrier(keep=("xg_all",), exclude=(g.cc2,))
    cab = _Carve(AB.shape[1])
    o_wo = cab.take(32768)
    o_xgt = [cab.take(4096), cab.take(4096)]
    o_xgT = [cab.take(4096), cab.take(4096)]
    op("pool", lambda e: (e.cc_rank_ld(g.rank_reg, [[0, 1, 2, 3], [4, 5, 6, 7]]),
                          e.reg_mul(g.roff_reg, g.rank_reg, TSHARD * TSHARD))[-1],
       reads=[], writes=["roff"], dbg="rank")
    for wh in range(2):
        dma(nch("cw"), lambda e, hf=wh: e.dma_start(
            out=AB[:, o_wo:o_wo + 32768].rearrange("p (a n) -> p a n", a=32),
            in_=g.wout_d[:, 1024 * hf:1024 * hf + 1024].rearrange("(a p) n -> p a n", p=128)),
            reads=["wo"], writes=["wo"], dbg=f"lwo{wh}", cost=46000)
        op("dve", lambda e: e.tensor_scalar(AB[:, o_wo:o_wo + 32768], AB[:, o_wo:o_wo + 32768],
                                            g.normw_sb[:, 0:1], None, ALU.mult),
           reads=["wo", "normw"], writes=["wo"], dbg=f"nwo{wh}")
        for ts_ in range(8):
            tp2 = ts_ % 2
            xgt = ab(o_xgt[tp2], 4096)
            rxgt = f"xgt{tp2}"
            rxgT = f"xgT{tp2}"

            def _ldxg(e, t_=ts_, d=xgt.rearrange("p (j e2) -> p j e2", j=4)):
                e.reg_add(g.tmp_reg, g.roff_reg, (128 * t_) * 1024)
                return e.dma_start(out=d, in_=bass.AP(g.xg_all_h, g.tmp_reg,
                                                      [[1024, 128], [4096 * 1024, 4], [1, 1024]]))
            dma(nch("cx"), _ldxg, reads=["xg_all", "roff"], writes=[rxgt], dbg=f"lxg{wh}.{ts_}")
            for ch in range(32):
                op("pe", lambda e, o=psb[0][:, 128 * (ch % 8):128 * (ch % 8) + 128],
                   i=xgt[:, 128 * ch:128 * ch + 128]:
                   e.transpose(o, i, g.iden128[:, :]),
                   reads=[rxgt, "iden128"], writes=["psb0"], dbg=f"Txg{wh}.{ts_}.{ch}")
                if ch % 8 == 7:
                    op("dve", lambda e, d=ab(o_xgT[tp2] + 1024 * (ch // 8), 1024),
                       p=psb[0][:, 0:1024]: e.tensor_copy(d, p),
                       reads=["psb0"], writes=[rxgT], dbg=f"Texg{wh}.{ts_}.{ch // 8}")
            ys = af(o_ys[tp2], 2048)[:, 0:1024]
            rys = f"ys{tp2}"
            for dt in range(2):
                pd = psf[dt % 2]
                rpd = f"psf{dt % 2}"
                for ch in range(32):
                    op("pe", lambda e, p=pd[:, 0:512],
                       l=ab(o_xgT[tp2], 4096)[:, 128 * ch:128 * ch + 128],
                       r=AB[:, o_wo + 1024 * ch + 512 * dt:o_wo + 1024 * ch + 512 * dt + 512],
                       s=(ch == 0), t=(ch == 31): e.matmul(p, l, r, start=s, stop=t),
                       reads=[rxgT, "wo"], writes=[rpd], dbg=f"wo{wh}.{ts_}.{dt}.{ch}", cost=230)
                op("dve", lambda e, o=ys[:, 512 * dt:512 * dt + 512], p=pd[:, 0:512]:
                   e.tensor_copy(o, p),
                   reads=[rpd], writes=[rys], dbg=f"woe{wh}.{ts_}.{dt}")
            dma(nch("cy"), lambda e, t_=ts_, hf=wh, d=ys: e.dma_start(
                out=d, in_=g.x_d[128 * t_:128 * t_ + 128, 1024 * hf:1024 * hf + 1024],
                accum_op=ALU.add),
                reads=[rys], writes=[rys], dbg=f"wax{wh}.{ts_}")
            dma(nch("cy"), lambda e, t_=ts_, hf=wh, d=ys: e.dma_start(
                out=d, in_=g.y_dram[128 * t_:128 * t_ + 128, 1024 * hf:1024 * hf + 1024],
                accum_op=ALU.add),
                reads=[rys, f"y{ts_}"], writes=[rys], dbg=f"way{wh}.{ts_}")
            dma(nch("cy"), lambda e, t_=ts_, hf=wh, s=ys: e.dma_start(
                out=g.out_d[128 * t_:128 * t_ + 128, 1024 * hf:1024 * hf + 1024], in_=s),
                reads=[rys], writes=["outd"], dbg=f"fout{wh}.{ts_}")
    sched.barrier()



_CACHE = {}


def _consts():
    iden128 = np.eye(128, dtype=np.float32)
    ones = np.ones((128, 128), dtype=np.float32)
    sel = np.zeros((128, 256), dtype=np.float32)
    p = np.arange(128)
    sel[:, 0:128] = ((p % 4) < 2).astype(np.float32)[:, None]
    sel[:, 128:256] = ((p % 4) >= 2).astype(np.float32)[:, None]
    s128 = np.arange(128)[:, None]
    t128 = np.arange(128)[None, :]
    tril128 = (s128 <= t128).astype(np.float32)
    mask128 = np.where(s128 <= t128, 0.0, NEG_BIG).astype(np.float32)
    strict128 = np.where(s128 < t128, -1.0, 0.0).astype(np.float32)
    return dict(c_iden128=iden128, c_ones=ones, c_sel=sel,
                c_tril128=tril128, c_mask128=mask128, c_strict128=strict128,
                c_bias=np.tile(np.array([[1e-5, 1e-6, 1.0, 1.6e-5, 4e-6]], dtype=np.float32), (128, 1)))


def kernel(L0_attn_hidden_states, L0_attn_W_qkvz, L0_attn_W_ba, L0_attn_conv_w,
           L0_attn_dt_bias, L0_attn_A_log, L0_attn_norm_weight, L0_attn_W_out,
           L0_ffn_router_w, L0_ffn_wg, L0_ffn_wu, L0_ffn_wd, L0_ffn_sg,
           L0_ffn_su, L0_ffn_sd):
    x = np.asarray(L0_attn_hidden_states, dtype=np.float32)
    wq = np.asarray(L0_attn_W_qkvz, dtype=np.float32)
    wba = np.asarray(L0_attn_W_ba, dtype=np.float32)
    cw = np.asarray(L0_attn_conv_w, dtype=np.float32)
    dtb = np.asarray(L0_attn_dt_bias, dtype=np.float32)
    alog = np.asarray(L0_attn_A_log, dtype=np.float32)
    nw = np.asarray(L0_attn_norm_weight, dtype=np.float32)
    wout = np.asarray(L0_attn_W_out, dtype=np.float32)
    rw = np.asarray(L0_ffn_router_w, dtype=np.float32)
    wg = np.asarray(L0_ffn_wg, dtype=np.float32)
    wu = np.asarray(L0_ffn_wu, dtype=np.float32)
    wd = np.asarray(L0_ffn_wd, dtype=np.float32)
    sg = np.asarray(L0_ffn_sg, dtype=np.float32)
    su = np.asarray(L0_ffn_su, dtype=np.float32)
    sd = np.asarray(L0_ffn_sd, dtype=np.float32)

    if "nc" not in _CACHE:
        _CACHE["nc"] = build_program()
    nc, sched = _CACHE["nc"]

    consts = _consts()
    in_maps = []
    for c in range(8):
        b, r = c // 4, c % 4
        qs, ks, vs, zs = [], [], [], []
        for j in range(4):
            base = 768 * (4 * r + j)
            qs.append(wq[:, base:base + 128])
            ks.append(wq[:, base + 128:base + 256])
            vs.append(wq[:, base + 256:base + 512])
            zs.append(wq[:, base + 512:base + 768])
        wq_sl = np.ascontiguousarray(np.concatenate(qs + ks + vs + zs, axis=1))
        cw_rows = np.concatenate([cw[512 * r:512 * r + 512],
                                  cw[2048 + 512 * r:2048 + 512 * r + 512],
                                  cw[4096 + 1024 * r:4096 + 1024 * r + 1024]], axis=0)
        convw_sl = np.ascontiguousarray(cw_rows.reshape(16, 128, 4).transpose(1, 0, 2).reshape(128, 64))
        m = dict(consts)
        m.update(
            x_sh=np.ascontiguousarray(x[b, TSHARD * r:TSHARD * r + TSHARD, :]),
            wq_sl=wq_sl,
            wba_sl=np.ascontiguousarray(wba[:, 16 * r:16 * r + 16]),
            convw_sl=convw_sl,
            dtb_sl=np.ascontiguousarray(dtb[8 * r:8 * r + 8].reshape(1, 8)),
            alog_sl=np.ascontiguousarray(alog[8 * r:8 * r + 8].reshape(1, 8)),
            normw=np.ascontiguousarray(nw.reshape(128, 1)),
            wout=wout, router_w=rw, wg=wg, wu=wu, wd=wd, sg=sg, su=su, sd=sd,
        )
        in_maps.append(m)

    res = run_bass_kernel_spmd(nc, in_maps, core_ids=list(range(8)))
    out = np.empty((B, T, D), dtype=np.float32)
    for c in range(8):
        b, r = c // 4, c % 4
        out[b, TSHARD * r:TSHARD * r + TSHARD, :] = res.results[c]["out_sh"]
    return out


# version 15 (final)
# speedup vs baseline: 1.0167x; 1.0167x over previous
import contextlib
import sys

import numpy as np

sys.path.insert(0, "/opt/trn_rl_repo")

import concourse.bass as bass
import concourse.mybir as mybir
from concourse.bass_utils import run_bass_kernel_spmd

F32 = mybir.dt.float32
BF16 = mybir.dt.bfloat16
AF = mybir.ActivationFunctionType
ALU = mybir.AluOpType
AX = mybir.AxisListType

B, T, D = 2, 4096, 2048
HK, HV, DK, DV = 16, 32, 128, 128
CH = 128
NCH = T // CH
SCALE = float(DK) ** -0.5
TSHARD = 1024
TT = 512
NTT = T // TT
CPT = TT // CH
HKC = 4
HVC = 8
E_EXP = 8
HEXP = 1408
HSH = 2816
NEG_BIG = -float(2.0 ** 30)

ENGINES = ("pe", "dve", "act", "pool", "sp")


class _Op:
    __slots__ = ("engine", "emit", "deps", "odeps", "idx", "inc_no", "dbg", "is_dma",
                 "cost", "aset", "prio", "sched_pos")

    def __init__(self, engine, emit, deps, idx, dbg="", is_dma=False, cost=None, aset=None):
        self.engine = engine
        self.emit = emit
        self.deps = deps
        self.odeps = []
        self.idx = idx
        self.inc_no = None
        self.dbg = dbg
        self.is_dma = is_dma
        self.cost = cost
        self.aset = aset
        self.prio = 0.0
        self.sched_pos = None


class _Chan:

    def __init__(self, queue, sem_name, inc=16):
        self.queue = queue
        self.sem_name = sem_name
        self.sem = None
        self.issued = 0
        self.inc = inc


class Sched:

    def __init__(self, nc):
        self.nc = nc
        self.ops = {e: [] for e in ENGINES}
        self.last_w = {}
        self.readers = {}
        self.chans = {}
        self._chan_rr = {}
        self.ctok = {}
        self._bar = {}
        self._bar_start = {e: 0 for e in ENGINES}
        self.n_emitted_waits = 0

    def add_chan_pool(self, queue, name, n):
        pool = [_Chan(queue, f"{name}{i}") for i in range(n)]
        self._chan_rr[name] = [pool, 0]
        return pool

    def next_chan(self, name):
        pool, i = self._chan_rr[name]
        self._chan_rr[name][1] = (i + 1) % len(pool)
        return pool[i]

    def _dep_tokens(self, reads, writes, engine):
        deps = []
        for r in reads:
            t = self.last_w.get(r)
            if t is not None:
                deps.append(t)
        for r in writes:
            t = self.last_w.get(r)
            if t is not None:
                deps.append(t)
            for tok in self.readers.get(r, {}).values():
                deps.append(tok)
        out, odep = [], []
        seen = set()
        for d in deps:
            if d in seen:
                continue
            seen.add(d)
            if d[0] == "e" and d[1] == "pe" and engine == "pe":
                odep.append(d)
            else:
                out.append(d)
        return out, odep

    def _record_access(self, token, reads, writes):
        for r in writes:
            self.last_w[r] = token
            self.readers[r] = {}
        for r in reads:
            d = self.readers.setdefault(r, {})
            d[token] = token

    def op(self, engine, emit, reads=(), writes=(), dbg="", cost=None, aset=None):
        deps, odeps = self._dep_tokens(reads, writes, engine)
        o = _Op(engine, emit, deps, len(self.ops[engine]), dbg, cost=cost, aset=aset)
        o.odeps = odeps
        if engine in self._bar:
            o.odeps.append(self._bar[engine])
        self.ops[engine].append(o)
        token = ("e", engine, o.idx)
        self._record_access(token, reads, writes)
        return token

    def dma(self, chan, emit, reads=(), writes=(), dbg="", cost=None):
        deps, odeps = self._dep_tokens(reads, writes, chan.queue)
        if chan.issued > 0:
            deps.append(("c", chan, chan.inc * chan.issued))
        chan.issued += 1
        total = chan.inc * chan.issued
        o = _Op(chan.queue, lambda eng, _e=emit, _c=chan: (
                _e(eng).then_inc(_c.sem, _c.inc) if _c.inc != 1 else _e(eng).then_inc(_c.sem)),
                deps, len(self.ops[chan.queue]), dbg, is_dma=True, cost=cost)
        o.odeps = odeps
        if chan.queue in self._bar:
            o.odeps.append(self._bar[chan.queue])
        self.ops[chan.queue].append(o)
        token = ("c", chan, total)
        self.ctok[(chan, total)] = o
        self._record_access(token, reads, writes)
        return token

    def barrier(self, keep=(), exclude=()):
        arrive = {}
        for e in ENGINES:
            o = _Op(e, lambda eng: eng.nop(nofuse=True), [], len(self.ops[e]), "bar_arr")
            o.odeps = [("e", e, i) for i in range(self._bar_start[e], o.idx)]
            self.ops[e].append(o)
            arrive[e] = ("e", e, o.idx)
        chan_lasts = []
        for pool, _ in self._chan_rr.values():
            for c in pool:
                if c.issued and c not in exclude:
                    chan_lasts.append(("c", c, c.inc * c.issued))
        for e in ENGINES:
            deps = [arrive[f] for f in ENGINES if f != e] + chan_lasts
            o = _Op(e, lambda eng: eng.nop(nofuse=True), deps, len(self.ops[e]), "barrier")
            o.odeps = [arrive[e]]
            self.ops[e].append(o)
            self._bar[e] = ("e", e, o.idx)
            self._bar_start[e] = len(self.ops[e])
        kept = {r: self.last_w[r] for r in keep if r in self.last_w}
        self.last_w = dict(kept)
        self.readers = {}

    DEF_COST = {"pe": 90, "dve": 330, "act": 420, "pool": 700, "sp": 250}
    DMA_ISSUE = {"pool": 900, "sp": 350}
    DMA_LAT = 1800

    def _tok_op(self, d):
        if d[0] == "e":
            return self.ops[d[1]][d[2]]
        return self.ctok[(d[1], d[2])]

    def schedule(self):
        import heapq
        allops = []
        for e in ENGINES:
            allops.extend(self.ops[e])
        nsucc = {}
        indeg = {}
        succs = {}
        for o in allops:
            indeg[id(o)] = 0
            succs[id(o)] = []
        for o in allops:
            for d in list(o.deps) + list(o.odeps):
                p = self._tok_op(d)
                succs[id(p)].append(o)
                indeg[id(o)] += 1
        outdeg = {id(o): len(succs[id(o)]) for o in allops}
        prio = {id(o): 0.0 for o in allops}
        stack = [o for o in allops if outdeg[id(o)] == 0]
        opmap = {id(o): o for o in allops}
        preds = {id(o): [self._tok_op(d) for d in list(o.deps) + list(o.odeps)] for o in allops}
        while stack:
            o = stack.pop()
            c = o.cost if o.cost is not None else (self.DMA_ISSUE[o.engine] + self.DMA_LAT
                                                  if o.is_dma else self.DEF_COST[o.engine])
            for p in preds[id(o)]:
                v = prio[id(o)] + c
                if v > prio[id(p)]:
                    prio[id(p)] = v
                outdeg[id(p)] -= 1
                if outdeg[id(p)] == 0:
                    stack.append(p)
        ready = {e: [] for e in ENGINES}
        seq = 0
        for o in allops:
            if indeg[id(o)] == 0:
                heapq.heappush(ready[o.engine], (-prio[id(o)], seq, o))
                seq += 1
        efree = {e: 0.0 for e in ENGINES}
        done = {}
        data_done = {}
        order = {e: [] for e in ENGINES}
        act_set = [None]
        n_remaining = len(allops)
        while n_remaining:
            best = None
            for e in ENGINES:
                if not ready[e]:
                    continue
                _, _, o = ready[e][0]
                rt = 0.0
                for d in list(o.deps) + list(o.odeps):
                    p = self._tok_op(d)
                    rt = max(rt, data_done[id(p)])
                st = max(efree[e], rt)
                key = (st, -prio[id(o)])
                if best is None or key < best[0]:
                    best = (key, e, o, st)
            _, e, o, st = best
            heapq.heappop(ready[e])
            occ = o.cost if (o.cost is not None and not o.is_dma) else (
                self.DMA_ISSUE[e] if o.is_dma else self.DEF_COST[e])
            if e == "act" and o.aset is not None:
                if act_set[0] is not None and act_set[0] != o.aset:
                    occ += 2700.0
                act_set[0] = o.aset
            fin = st + occ
            efree[e] = fin
            done[id(o)] = fin
            data_done[id(o)] = fin + ((o.cost if o.cost is not None else 0) + self.DMA_LAT
                                      if o.is_dma else 0.0)
            order[e].append(o)
            o.sched_pos = len(order[e]) - 1
            n_remaining -= 1
            for s in succs[id(o)]:
                indeg[id(s)] -= 1
                if indeg[id(s)] == 0:
                    heapq.heappush(ready[s.engine], (-prio[id(s)], seq, s))
                    seq += 1
        self.est_span = max(done.values())
        return order

    def finalize(self, block, sems, chan_sems):
        order = self.schedule()
        for e in ENGINES:
            for o in self.ops[e]:
                for d in o.deps:
                    if d[0] == "e":
                        t = self.ops[d[1]][d[2]]
                        assert not t.is_dma, f"engine dep targets DMA op {t.dbg}"
                        if t.dbg != "bar_arr":
                            t.inc_no = True
        arr_carrier = {}
        for e in ENGINES:
            last_real = None
            for o in order[e]:
                if o.dbg == "bar_arr":
                    arr_carrier[(e, o.idx)] = last_real
                    if last_real is not None:
                        last_real.inc_no = True
                elif not o.is_dma and o.dbg != "barrier":
                    last_real = o
        inc_val = {}
        arr_val = {}
        for e in ENGINES:
            n = 0
            for o in order[e]:
                if o.dbg == "bar_arr":
                    car = arr_carrier[(e, o.idx)]
                    arr_val[(e, o.idx)] = inc_val[(e, car.idx)] if car is not None else 0
                if o.inc_no:
                    n += 1
                    inc_val[(e, o.idx)] = n
        self.inc_val = inc_val
        self.arr_val = arr_val
        for name, (pool, _) in self._chan_rr.items():
            for c in pool:
                c.sem = chan_sems[c.sem_name]

        def run_engine(ename):
            sem_self = sems[ename]

            def body(eng):
                vclock = {}
                for o in order[ename]:
                    for d in o.deps:
                        if d[0] == "e":
                            key = ("e", d[1])
                            t = self.ops[d[1]][d[2]]
                            val = (arr_val[(d[1], d[2])] if t.dbg == "bar_arr"
                                   else inc_val[(d[1], d[2])])
                            tgt = sems[d[1]]
                        else:
                            key = ("c", d[1].sem_name)
                            val = d[2]
                            tgt = d[1].sem
                        if vclock.get(key, 0) >= val:
                            continue
                        eng.wait_ge(tgt, val)
                        self.n_emitted_waits += 1
                        vclock[key] = val
                    ins = o.emit(eng)
                    if o.inc_no:
                        ins.then_inc(sem_self, 1)
            return body

        for e in ENGINES:
            name = {"pe": "tensor", "dve": "vector", "act": "scalar",
                    "pool": "gpsimd", "sp": "sync"}[e]
            getattr(block, name)(run_engine(e))

    def stats(self):
        return {e: len(self.ops[e]) for e in ENGINES}



def build_program(debug=False):
    nc = bass.Bass("TRN2", target_bir_lowering=False)

    def inp(name, shape):
        return nc.dram_tensor(name, list(shape), F32, kind="ExternalInput").ap()

    x_d = inp("x_sh", (TSHARD, D))
    wq_d = inp("wq_sl", (D, 3072))
    wba_d = inp("wba_sl", (D, 16))
    convw_d = inp("convw_sl", (128, 64))
    dtb_d = inp("dtb_sl", (1, 8))
    alog_d = inp("alog_sl", (1, 8))
    normw_d = inp("normw", (128, 1))
    wout_d = inp("wout", (HV * DV, D))
    rw_d = inp("router_w", (8, D))
    wg_d = inp("wg", (E_EXP, D, HEXP))
    wu_d = inp("wu", (E_EXP, D, HEXP))
    wd_d = inp("wd", (E_EXP, HEXP, D))
    sg_d = inp("sg", (D, HSH))
    su_d = inp("su", (D, HSH))
    sd_d = inp("sd", (HSH, D))
    iden128_d = inp("c_iden128", (128, 128))
    ones_d = inp("c_ones", (128, 128))
    sel_d = inp("c_sel", (128, 256))
    tril128_d = inp("c_tril128", (128, 128))
    mask128_d = inp("c_mask128", (128, 128))
    strict128_d = inp("c_strict128", (128, 128))
    cbias_d = inp("c_bias", (128, 5))

    out_d = nc.dram_tensor("out_sh", [TSHARD, D], F32, kind="ExternalOutput").ap()

    xn_con = nc.dram_tensor("xn_con", [D, TSHARD], BF16).ap()
    xn_all = nc.dram_tensor("xn_all", [4 * D, TSHARD], BF16).ap()
    xg_con = nc.dram_tensor("xg_con", [T, HVC * DV], BF16).ap()
    xg_all_h = nc.dram_tensor("xg_all", [4 * T, HVC * DV], BF16)
    xg_all = xg_all_h.ap()
    rank_reg = nc.gpsimd.alloc_register("rank_reg")
    roff_reg = nc.gpsimd.alloc_register("roff_reg")
    tmp_reg = nc.gpsimd.alloc_register("tmp_reg")
    y_dram = nc.dram_tensor("y_dram", [TSHARD, D], F32).ap()
    dbg_qts = nc.dram_tensor("dbg_qts", [128, 2048], BF16).ap() if debug else None
    dbg_kts = nc.dram_tensor("dbg_kts", [128, 2048], BF16).ap() if debug else None
    dbg_ob = nc.dram_tensor("dbg_ob", [128, 4096], F32).ap() if debug else None
    dbg_u = nc.dram_tensor("dbg_u", [128, 4096], BF16).ap() if debug else None
    dbg_E = nc.dram_tensor("dbg_E", [128, 4096], F32).ap() if debug else None
    dbg_r = nc.dram_tensor("dbg_r", [128, 4096], BF16).ap() if debug else None
    dbg_S = nc.dram_tensor("dbg_S", [128, 1024], BF16).ap() if debug else None
    dbg_Ss = nc.dram_tensor("dbg_Ss", [128, 1024], F32).ap() if debug else None
    dbg_kd = nc.dram_tensor("dbg_kd", [128, 1024], BF16).ap() if debug else None
    dbg_vts = nc.dram_tensor("dbg_vts", [128, 4096], BF16).ap() if debug else None
    dbg_zts = nc.dram_tensor("dbg_zts", [128, 4096], BF16).ap() if debug else None

    sched = Sched(nc)
    ch_w = sched.add_chan_pool("pool", "cw", 6)
    ch_y = sched.add_chan_pool("pool", "cy", 4)
    ch_s = sched.add_chan_pool("sp", "cs", 8)
    ch_x = sched.add_chan_pool("pool", "cx", 2)
    cc1 = _Chan("pool", "cc1", inc=1)
    cc2 = _Chan("pool", "cc2", inc=1)
    sched._chan_rr["cc1"] = [[cc1], 0]
    sched._chan_rr["cc2"] = [[cc2], 0]

    es = contextlib.ExitStack()
    with es:
        def sb(name, p, cols, dt):
            return es.enter_context(nc.sbuf_tensor(name, [p, cols], dt))

        iden128 = sb("iden128", 128, 128, BF16)
        onesb = sb("onesb", 128, 64, BF16)
        onesf = sb("onesf", 128, 128, F32)
        selb = sb("selb", 128, 256, BF16)
        tril128 = sb("tril128", 128, 128, F32)
        mask128 = sb("mask128", 128, 128, BF16)
        strict128 = sb("strict128", 128, 128, F32)
        cstc = sb("cstc", 128, 5, F32)
        ln_scr = sb("ln_scr", 128, 8, F32)
        rsc = sb("rsc", 128, 48, F32)
        convw_sb = sb("convw_sb", 128, 64, F32)
        dtb_sb = sb("dtb_sb", 1, 8, F32)
        alog_sb = sb("alog_sb", 1, 8, F32)
        nrow_sb = sb("nrow_sb", 1, 8, F32)
        dtb8 = sb("dtb8", 128, 8, F32)
        ea8 = sb("ea8", 128, 8, F32)
        normw_sb = sb("normw_sb", 128, 1, F32)
        gates_sb = sb("gates_sb", 128, 64, F32)
        ba_tok = sb("ba_tok", 128, 512, F32)
        b8 = sb("b8", 128, 256, F32)
        a8 = sb("a8", 128, 256, F32)
        beta_tok = sb("beta_tok", 128, 256, F32)
        g_tok = sb("g_tok", 128, 256, F32)
        G_tok = sb("G_tok", 128, 256, F32)
        rows2 = [sb(f"rows2_{i}", 128, 4096, BF16) for i in range(2)]
        negG_sl = sb("negG_sl", 128, 256, F32)
        eG8_sl = sb("eG8_sl", 128, 256, F32)
        neG_sl = sb("neG_sl", 128, 256, F32)
        beta_sl = sb("beta_sl", 128, 256, F32)
        betaS_sl = sb("betaS_sl", 128, 256, F32)
        egC_all = sb("egC_all", 128, 512, F32)
        gCrow = sb("gCrow", 1, 512, F32)
        S_sb = sb("S_sb", 128, HVC * DV, F32)
        S_bf = sb("S_bf", 128, HVC * DV, BF16)
        ssqQ = sb("ssqQ", 128, 32, F32)
        ssqK = sb("ssqK", 128, 32, F32)
        rsqQ = sb("rsqQ", 128, 32, F32)
        rsqK = sb("rsqK", 128, 32, F32)
        msq = [sb(f"msq{i}", 128, 32, F32) for i in range(2)]
        rstd = [sb(f"rstd{i}", 128, 32, F32) for i in range(2)]
        wba_sb = sb("wba_sb", 128, 256, BF16)
        rwT = sb("rwT", 128, 128, BF16)
        AB = sb("AB", 128, 65536, BF16)
        AFA = sb("AFA", 128, 4864, F32)

        psf = [es.enter_context(nc.psum_tensor(f"psf{i}", [128, 512], F32)) for i in range(6)]
        psb = [es.enter_context(nc.psum_tensor(f"psb{i}", [128, 1024], BF16)) for i in range(2)]

        sems = {e: es.enter_context(nc.semaphore(f"sem_{e}")) for e in ENGINES}
        chan_sems = {}
        for pool_, _ in sched._chan_rr.values():
            for c in pool_:
                chan_sems[c.sem_name] = es.enter_context(nc.semaphore(c.sem_name))

        ctx = dict(locals())
        _build_body(sched, ctx)

        with nc.Block() as block:
            sched.finalize(block, sems, chan_sems)

    return nc, sched


class _Carve:
    def __init__(self, cols):
        self.cols = cols
        self.off = 0

    def take(self, n, align=64):
        o = (self.off + align - 1) // align * align
        assert o + n <= self.cols, f"arena overflow: {o}+{n} > {self.cols}"
        self.off = o + n
        return o


def _build_body(sched, C):
    from types import SimpleNamespace
    g = SimpleNamespace(**C)
    nc = g.nc
    op, dma, nch = sched.op, sched.dma, sched.next_chan

    def _rowslot(hv):
        if hv < 6:
            return 0, 32 * (hv // 2), hv % 2
        return 1, 0, hv - 6
    AB, AFW = g.AB, g.AFA
    psf, psb = g.psf, g.psb

    def cast_load(dst_ap, src_ap, rg, rr=(), dbg=""):
        return dma(nch("cw"), lambda e, d=dst_ap, s=src_ap: e.dma_start(out=d, in_=s),
                   reads=rr, writes=[rg], dbg=dbg)

    def plain_load(dst_ap, src_ap, rg, rr=(), dbg=""):
        return dma(nch("cs"), lambda e, d=dst_ap, s=src_ap: e.dma_start(out=d, in_=s),
                   reads=rr, writes=[rg], dbg=dbg)

    cast_load(g.iden128[:, :], g.iden128_d, "iden128")
    cast_load(g.onesb[:, :], g.ones_d[:, 0:64], "onesb")
    plain_load(g.onesf[:, :], g.ones_d, "onesf")
    cast_load(g.selb[:, :], g.sel_d, "selb")
    plain_load(g.tril128[:, :], g.tril128_d, "tril128")
    cast_load(g.mask128[:, :], g.mask128_d, "mask128")
    plain_load(g.strict128[:, :], g.strict128_d, "strict128")
    plain_load(g.cstc[:, :], g.cbias_d, "cstc")
    plain_load(g.convw_sb[:, :], g.convw_d, "convw")
    plain_load(g.dtb_sb[:, :], g.dtb_d, "dtb")
    plain_load(g.alog_sb[:, :], g.alog_d, "alog")
    plain_load(g.normw_sb[:, :], g.normw_d, "normw")
    cast_load(g.wba_sb[:, :].rearrange("p (c n) -> p c n", c=16),
              g.wba_d.rearrange("(c p) n -> p c n", p=128), "wba")

    cab = _Carve(AB.shape[1])
    caf = _Carve(AFW.shape[1])
    o_xnl = cab.take(16 * TSHARD)
    o_xt = [caf.take(D), caf.take(D)]
    o_xnbf = [cab.take(D), cab.take(D)]
    o_junk = cab.take(D)
    o_rwt = cab.take(D)
    o_ghi = cab.take(4096)
    o_glo = cab.take(4096)

    xnl = AB[:, o_xnl:o_xnl + 16 * TSHARD]
    lns = g.ln_scr

    for u in range(8):
        xt = AFW[:, o_xt[u % 2]:o_xt[u % 2] + D]
        rx = f"xt{u % 2}"
        rxa, rxb = rx + "a", rx + "b"
        plain_load(xt[0:64, :], g.x_d[128 * u:128 * u + 64, :], rxa, dbg=f"ldx{u}a")
        plain_load(xt[64:128, :], g.x_d[128 * u + 64:128 * u + 128, :], rxb, dbg=f"ldx{u}b")
        sc = lns[:, 4 * (u % 2):4 * (u % 2) + 4]
        rs = f"lns{u % 2}"
        op("dve", lambda e, a=sc[:, 0:1], b=xt: e.tensor_reduce(a, b, AX.X, ALU.add, False, False, True),
           reads=[rxa, rxb], writes=[rs], dbg=f"lnsum{u}")
        op("dve", lambda e, a=sc[:, 0:1]: e.tensor_scalar(a, a, 1.0 / D, None, ALU.mult),
           reads=[rs], writes=[rs])
        op("dve", lambda e, a=xt, m=sc[:, 0:1]: e.tensor_scalar(a, a, m[:, 0:1], None, ALU.add),
           reads=[rxa, rxb, rs], writes=[rxa, rxb])
        op("dve", lambda e, jk=AB[:, o_junk:o_junk + D], a=xt, s=sc[:, 1:2]:
           e.scalar_tensor_tensor(jk, a, 1.0, a, ALU.mult, ALU.mult, accum_out=s),
           reads=[rxa, rxb], writes=["junk", rs])
        op("act", lambda e, s=sc[:, 1:2], t=sc[:, 2:3]: e.activation(t, s, AF.Ln, bias=g.cstc[:, 0:1], scale=1.0 / D),
           reads=[rs, "cstc"], writes=[rs])
        op("act", lambda e, t=sc[:, 2:3]: e.activation(t, t, AF.Exp, bias=0.0, scale=-0.5),
           reads=[rs], writes=[rs])
        xnb = AB[:, o_xnbf[u % 2]:o_xnbf[u % 2] + D]
        rxn = f"xnbf{u % 2}"
        op("dve", lambda e, a=xnb, b=xt, s=sc[:, 2:3]: e.tensor_scalar(a, b, s[:, 0:1], None, ALU.mult),
           reads=[rxa, rxb, rs], writes=[rxn])
        for half in range(2):
            pb = psb[half]
            for d8 in range(8):
                dd = 8 * half + d8
                op("pe", lambda e, o=pb[:, 128 * d8:128 * d8 + 128], i=xnb[:, 128 * dd:128 * dd + 128],
                   idn=g.iden128[:, :]: e.transpose(o, i, idn),
                   reads=[rxn, "iden128"], writes=[f"psb{half}"], dbg=f"xnT{u}.{dd}")
            out_ap = AB[:, o_xnl:o_xnl + 16384].rearrange("p (d t) -> p d t", d=16)[
                :, 8 * half:8 * half + 8, 128 * u:128 * u + 128]
            op("dve", lambda e, a=out_ap, b=pb[:, :]: e.tensor_copy(a, b.rearrange("p (d t) -> p d t", d=8)),
               reads=[f"psb{half}"], writes=["xnl"], dbg=f"xnTe{u}.{half}")

    dma(nch("cs"), lambda e: e.dma_start(
        out=g.xn_con.rearrange("(c p) t -> p c t", p=128),
        in_=AB[:, o_xnl:o_xnl + 16384].rearrange("p (c t) -> p c t", c=16)),
        reads=["xnl"], writes=["xn_con"], dbg="st_xncon")
    dma(g.cc1, lambda e: e.collective_compute(
        "AllGather", ALU.bypass,
        replica_groups=[[0, 1, 2, 3], [4, 5, 6, 7]],
        ins=[g.xn_con.opt()], outs=[g.xn_all.opt()]),
        reads=["xn_con"], writes=["xn_all"], dbg="AG1", cost=60000)

    rwt = AB[:, o_rwt:o_rwt + D]
    cast_load(rwt[0:8, :], g.rw_d, "rwtmp")
    for d16 in range(16):
        op("pe", lambda e, o=psb[0][:, 8 * d16:8 * d16 + 8], i=rwt[0:8, 128 * d16:128 * d16 + 128],
           idn=g.iden128[0:8, 0:8]: e.transpose(o, i, idn),
           reads=["rwtmp", "iden128"], writes=["psb0"], dbg=f"rwT{d16}")
    op("dve", lambda e: e.tensor_copy(g.rwT[:, :], psb[0][:, 0:128]),
       reads=["psb0"], writes=["rwT"])

    for u in range(8):
        pg = psf[0][:, 0:8]
        for d16 in range(16):
            op("pe", lambda e, p=pg, l=AB[:, o_xnl + 1024 * d16 + 128 * u:o_xnl + 1024 * d16 + 128 * u + 128],
               r=g.rwT[:, 8 * d16:8 * d16 + 8], s=(d16 == 0), t=(d16 == 15):
               e.matmul(p, l, r, start=s, stop=t),
               reads=["xnl", "rwT"], writes=["psf0"], dbg=f"rmm{u}.{d16}")
        sc = g.rsc
        rg = "rsc"
        op("dve", lambda e, p=pg: e.tensor_reduce(sc[:, 8:9], p, AX.X, ALU.max, False, False, True),
           reads=["psf0"], writes=[rg])
        op("act", lambda e, p=pg: e.activation(sc[:, 0:8], p, AF.Exp, bias=sc[:, 8:9], scale=1.0),
           reads=["psf0", rg], writes=[rg + "e"])
        op("dve", lambda e: e.tensor_reduce(sc[:, 9:10], sc[:, 0:8], AX.X, ALU.add),
           reads=[rg + "e"], writes=[rg + "s"])
        op("dve", lambda e: e.reciprocal(sc[:, 10:11], sc[:, 9:10]),
           reads=[rg + "s"], writes=[rg + "s"])
        op("dve", lambda e: e.tensor_scalar(sc[:, 0:8], sc[:, 0:8], sc[:, 10:11], None, ALU.mult),
           reads=[rg + "e", rg + "s"], writes=[rg + "e"])
        op("dve", lambda e: e.tensor_reduce(sc[:, 11:12], sc[:, 0:8], AX.X, ALU.max),
           reads=[rg + "e"], writes=[rg + "m"])
        op("dve", lambda e: e.tensor_scalar(sc[:, 16:24], sc[:, 0:8], sc[:, 11:12], None, ALU.is_lt),
           reads=[rg + "e", rg + "m"], writes=[rg + "t"])
        op("dve", lambda e: e.tensor_tensor(sc[:, 16:24], sc[:, 16:24], sc[:, 0:8], ALU.mult),
           reads=[rg + "t", rg + "e"], writes=[rg + "t"])
        op("dve", lambda e: e.tensor_reduce(sc[:, 12:13], sc[:, 16:24], AX.X, ALU.max),
           reads=[rg + "t"], writes=[rg + "m2"])
        op("dve", lambda e: e.tensor_scalar(sc[:, 24:32], sc[:, 0:8], sc[:, 12:13], None, ALU.is_ge),
           reads=[rg + "e", rg + "m2"], writes=[rg + "k"])
        op("dve", lambda e: e.tensor_tensor(sc[:, 24:32], sc[:, 24:32], sc[:, 0:8], ALU.mult),
           reads=[rg + "k", rg + "e"], writes=[rg + "k"])
        op("dve", lambda e: e.tensor_reduce(sc[:, 13:14], sc[:, 24:32], AX.X, ALU.add),
           reads=[rg + "k"], writes=[rg + "s2"])
        op("dve", lambda e: e.tensor_scalar(sc[:, 13:14], sc[:, 13:14], 1e-20, None, ALU.add),
           reads=[rg + "s2"], writes=[rg + "s2"])
        op("dve", lambda e: e.reciprocal(sc[:, 14:15], sc[:, 13:14]),
           reads=[rg + "s2"], writes=[rg + "s2"])
        op("dve", lambda e, u_=u: e.tensor_scalar(g.gates_sb[:, 8 * u_:8 * u_ + 8], sc[:, 24:32],
                                                  sc[:, 14:15], None, ALU.mult),
           reads=[rg + "k", rg + "s2"], writes=["gates"], dbg=f"gate{u}")

    o_bat = [cab.take(2048), cab.take(2048)]
    for w in range(32):
        bt = AB[:, o_bat[w % 2]:o_bat[w % 2] + 2048]
        rb = f"bat{w % 2}"
        rb_blk = w // 8
        dma(nch("cs"), lambda e, d=bt.rearrange("p (c t) -> p c t", c=16), w_=w, rbk=rb_blk:
            e.dma_start(out=d, in_=g.xn_all[2048 * rbk:2048 * rbk + 2048,
                                            128 * (w_ % 8):128 * (w_ % 8) + 128].rearrange(
                "(c p) t -> p c t", p=128)),
            reads=["xn_all"], writes=[rb], dbg=f"ld_bat{w}")
        pg = psf[1][:, 0:16]
        for d16 in range(16):
            op("pe", lambda e, p=pg, l=bt[:, 128 * d16:128 * d16 + 128],
               r=g.wba_sb[:, 16 * d16:16 * d16 + 16], s=(d16 == 0), t=(d16 == 15):
               e.matmul(p, l, r, start=s, stop=t),
               reads=[rb, "wba"], writes=["psf1"], dbg=f"bamm{w}.{d16}")
        op("dve", lambda e, w_=w, p=pg: e.tensor_copy(g.ba_tok[:, 16 * w_:16 * w_ + 16], p),
           reads=["psf1"], writes=["ba_tok"], dbg=f"bae{w}")

    bav = g.ba_tok[:, :].rearrange("p (w h c) -> p w h c", w=32, c=4)
    op("dve", lambda e: e.tensor_copy(g.b8[:, :].rearrange("p (w h c) -> p w h c", w=32, c=2),
                                      bav[:, :, :, 0:2]), reads=["ba_tok"], writes=["b8"])
    op("dve", lambda e: e.tensor_copy(g.a8[:, :].rearrange("p (w h c) -> p w h c", w=32, c=2),
                                      bav[:, :, :, 2:4]), reads=["ba_tok"], writes=["a8"])
    op("act", lambda e: e.activation(g.beta_tok[:, :], g.b8[:, :], AF.Exp, bias=0.0, scale=-1.0),
       reads=["b8"], writes=["beta"])
    op("dve", lambda e: e.tensor_scalar(g.beta_tok[:, :], g.beta_tok[:, :], 1.0, None, ALU.add),
       reads=["beta"], writes=["beta"])
    op("dve", lambda e: e.reciprocal(g.beta_tok[:, :], g.beta_tok[:, :]),
       reads=["beta"], writes=["beta"])
    op("act", lambda e: e.activation(g.nrow_sb[:, :], g.alog_sb[:, :], AF.Exp),
       reads=["alog"], writes=["nrow"])
    op("dve", lambda e: e.tensor_scalar(g.nrow_sb[:, :], g.nrow_sb[:, :], -1.0, None, ALU.mult),
       reads=["nrow"], writes=["nrow"])
    op("pe", lambda e: e.matmul(psf[2][:, 0:8], g.onesf[0:1, :], g.dtb_sb[:, :], start=True, stop=True),
       reads=["onesf", "dtb"], writes=["psf2"])
    op("dve", lambda e: e.tensor_copy(g.dtb8[:, :], psf[2][:, 0:8]), reads=["psf2"], writes=["dtb8"])
    op("pe", lambda e: e.matmul(psf[2][:, 8:16], g.onesf[0:1, :], g.nrow_sb[:, :], start=True, stop=True),
       reads=["onesf", "nrow"], writes=["psf2"])
    op("dve", lambda e: e.tensor_copy(g.ea8[:, :], psf[2][:, 8:16]), reads=["psf2"], writes=["ea8"])
    a8v = g.a8[:, :].rearrange("p (w h) -> p w h", h=8)
    op("dve", lambda e: e.tensor_tensor(a8v, a8v, g.dtb8[:, :].unsqueeze(1).broadcast_to([128, 32, 8]),
                                        ALU.add), reads=["a8", "dtb8"], writes=["a8"])
    scA = AFW[:, 4096:4352]
    scB = AFW[:, 4352:4608]
    scC = AFW[:, 4608:4864]
    op("act", lambda e: e.activation(scA, g.a8[:, :], AF.Abs), reads=["a8"], writes=["spA", "xt0a", "xt0b"])
    op("act", lambda e: e.activation(scA, scA, AF.Exp, bias=0.0, scale=-1.0), reads=["spA"], writes=["spA"])
    op("act", lambda e: e.activation(scB, scA, AF.Ln, bias=g.cstc[:, 2:3]), reads=["spA", "cstc"], writes=["spB"])
    op("dve", lambda e: e.tensor_scalar(scC, g.a8[:, :], 0.0, None, ALU.max), reads=["a8"], writes=["spC", "xt0a", "xt0b"])
    op("dve", lambda e: e.tensor_tensor(scB, scB, scC, ALU.add), reads=["spB", "spC"], writes=["spB"])
    gv = g.g_tok[:, :].rearrange("p (w h) -> p w h", h=8)
    op("dve", lambda e: e.tensor_tensor(gv, scB.rearrange("p (w h) -> p w h", h=8),
                                        g.ea8[:, :].unsqueeze(1).broadcast_to([128, 32, 8]), ALU.mult),
       reads=["spB", "ea8"], writes=["g_tok"])

    G_rows = AFW[0:8, 0:4096]
    for c in range(NCH):
        pk = psf[3 + (c % 2)]
        rps = f"psf{3 + c % 2}"
        op("pe", lambda e, p=pk[:, 0:8], r=g.g_tok[:, 8 * c:8 * c + 8]:
           e.matmul(p, g.tril128[:, :], r, start=True, stop=True),
           reads=["g_tok", "tril128"], writes=[rps], dbg=f"Gc{c}")
        op("dve", lambda e, c_=c, p=pk: e.tensor_copy(g.G_tok[:, 8 * c_:8 * c_ + 8], p[:, 0:8]),
           reads=[rps], writes=["G_tok"], dbg=f"Ge{c}")
        op("pe", lambda e, p=pk[0:8, 128:256], l=g.g_tok[:, 8 * c:8 * c + 8]:
           e.matmul(p, l, g.tril128[:, :], start=True, stop=True),
           reads=["g_tok", "tril128"], writes=[rps], dbg=f"Gr{c}")
        op("dve", lambda e, c_=c, p=pk: e.tensor_copy(G_rows[:, 128 * c_:128 * c_ + 128], p[0:8, 128:256]),
           reads=[rps], writes=(["G_rows", "xt0a", "xt0b", "xt1a", "xt1b"] if c == 0 else ["G_rows"]),
           dbg=f"Gre{c}")
    GhiR = AB[0:8, o_ghi:o_ghi + 4096]
    GloR = AB[0:8, o_glo:o_glo + 4096]
    op("dve", lambda e: e.tensor_copy(GhiR, G_rows), reads=["G_rows"], writes=["GhiR"])
    op("dve", lambda e: e.tensor_tensor(GloR, G_rows, GhiR, ALU.subtract),
       reads=["G_rows", "GhiR"], writes=["GloR"])
    for hv in range(8):
        tid, gb, sl_ = _rowslot(hv)
        rr = gb + 2 * sl_
        dma(nch("cs"), lambda e, hv_=hv, t2=tid, r2=rr: e.dma_start(
            out=g.rows2[t2][r2:r2 + 1, :], in_=GhiR[hv_:hv_ + 1, :]),
            reads=["GhiR"], writes=["rows2"], dbg=f"rh{hv}")
        dma(nch("cs"), lambda e, hv_=hv, t2=tid, r2=rr: e.dma_start(
            out=g.rows2[t2][r2 + 1:r2 + 2, :], in_=GloR[hv_:hv_ + 1, :]),
            reads=["GloR"], writes=["rows2"], dbg=f"rl{hv}")
    scr2 = AFW[:, 4352:4608]
    op("dve", lambda e: e.tensor_scalar(g.negG_sl[:, :], g.G_tok[:, :], -1.0, None, ALU.mult),
       reads=["G_tok"], writes=["negG_sl"])
    op("act", lambda e: e.activation(scr2, g.G_tok[:, :], AF.Exp),
       reads=["G_tok"], writes=["scr2", "spB"])
    op("dve", lambda e: e.tensor_scalar(g.eG8_sl[:, :], scr2, SCALE, None, ALU.mult),
       reads=["scr2"], writes=["eG8_sl"])
    op("dve", lambda e: e.tensor_scalar(g.neG_sl[:, :], scr2, -1.0, None, ALU.mult),
       reads=["scr2"], writes=["neG_sl"])
    op("dve", lambda e: e.tensor_scalar(g.betaS_sl[:, :], g.beta_tok[:, :], SCALE, None, ALU.mult),
       reads=["beta"], writes=["betaS_sl"])
    dma(nch("cs"), lambda e: e.dma_start(out=g.gCrow[0:1, 0:256], in_=g.G_tok[127:128, :]),
        reads=["G_tok"], writes=["gCrow"], dbg="gC")
    op("pe", lambda e: e.matmul(psf[2][:, 0:256], g.onesf[0:1, :], g.gCrow[0:1, 0:256],
                                start=True, stop=True),
       reads=["onesf", "gCrow"], writes=["psf2"], dbg="egCmm")
    op("act", lambda e: e.activation(g.egC_all[:, 0:256], psf[2][:, 0:256], AF.Exp),
       reads=["psf2"], writes=["egC"], dbg="egCexp")

    sched.barrier()
    cab = _Carve(AB.shape[1])
    caf = _Carve(AFW.shape[1])
    o_wq = [cab.take(2048) for _ in range(3)]
    o_rhs = cab.take(8192)
    o_cv = cab.take(16 * 520)
    o_ms = [cab.take(512), cab.take(512)]
    o_tt = [cab.take(512), cab.take(512)]
    o_qts = [cab.take(2048), cab.take(2048)]
    o_kts = [cab.take(2048), cab.take(2048)]
    o_vts = cab.take(4096)
    o_zts = cab.take(4096)
    o_qh = [cab.take(512), cab.take(512)]
    o_kh = [cab.take(512), cab.take(512)]
    o_kd = [cab.take(1024), cab.take(1024)]
    o_NT = [cab.take(1024) for _ in range(6)]
    o_N = [cab.take(1024), cab.take(1024)]
    o_A = [cab.take(1024), cab.take(1024)]
    o_PE = cab.take(2048)
    o_yb = [cab.take(1024), cab.take(1024)]
    o_xg = cab.take(4096)
    o_jk = cab.take(1024)
    o_jk2 = cab.take(128)
    o_E = [caf.take(1024), caf.take(1024)]
    o_ob = caf.take(1024)
    o_oq = caf.take(1024)
    o_kc = caf.take(32)

    def ab(off, n):
        return AB[:, off:off + n]

    def af(off, n):
        return AFW[:, off:off + n]

    def nt_buf(k):
        return (ab(o_NT[k], 1024), f"NT_{k}") if k < 6 else (ab(o_PE, 1024), "PEX")

    for hv in range(HVC):
        op("dve", lambda e, h_=hv: e.memset(g.S_sb[:, 128 * h_:128 * h_ + 128], 0.0),
           reads=[], writes=[f"Ss{hv}"], dbg=f"S0.{hv}")
        op("dve", lambda e, h_=hv: e.memset(g.S_bf[:, 128 * h_:128 * h_ + 128], 0.0),
           reads=[], writes=[f"Sb{hv}"], dbg=f"Sbf0.{hv}")
    for ct in range(16):
        op("dve", lambda e, c_=ct: e.memset(ab(o_cv + 520 * c_, 3), 0.0),
           reads=[], writes=[f"cv{ct}"], dbg=f"cv0.{ct}")

    def gen_head(ti, part):
        par = ti % 2
        rrhs = "rhs"
        rhsb = ab(o_rhs, 8192)
        if part == 0:
            dma(nch("cs"), lambda e, t_=ti, d=rhsb.rearrange("p (c t) -> p c t", c=16): e.dma_start(
                out=d, in_=g.xn_all[2048 * (t_ // 2):2048 * (t_ // 2) + 2048,
                                    512 * (t_ % 2):512 * (t_ % 2) + 512].rearrange("(c p) t -> p c t", p=128)),
                reads=["xn_all"], writes=[rrhs], dbg=f"ldrhs{ti}", cost=6000)
        for ct in (range(8) if part == 0 else range(8, 24)):
            cpar = ct % 2
            rwq = f"wqt{ct % 3}"
            wqb = ab(o_wq[ct % 3], 2048)
            dma(nch("cw"), lambda e, c_=ct, d=wqb.rearrange("p (c n) -> p c n", c=16): e.dma_start(
                out=d, in_=g.wq_d[:, 128 * c_:128 * c_ + 128].rearrange("(c p) n -> p c n", p=128)),
                reads=[], writes=[rwq], dbg=f"ldwq{ti}.{ct}", cost=3000)
            pq = psf[cpar]
            rps = f"psf{cpar}"
            for d16 in range(16):
                op("pe", lambda e, p=pq[:, 0:512], l=wqb[:, 128 * d16:128 * d16 + 128],
                   r=rhsb.rearrange("p (c t) -> p c t", c=16)[:, d16, :],
                   s=(d16 == 0), t=(d16 == 15): e.matmul(p, l, r, start=s, stop=t),
                   reads=[rwq, rrhs], writes=[rps], dbg=f"qmm{ti}.{ct}.{d16}", cost=230)
            tt_ = ab(o_tt[ct % 2], 512)
            rt = f"tt{ct % 2}"
            ms = ab(o_ms[ct % 2], 512)
            rm = f"ms{ct % 2}"
            if ct < 16:
                rcv = f"cv{ct}"
                cv = ab(o_cv + 520 * ct, 520)
                if ti > 0:
                    op("dve", lambda e, c=cv: e.tensor_copy(c[:, 0:3], c[:, 512:515]),
                       reads=[rcv], writes=[rcv], dbg=f"hist{ti}.{ct}")
                op("dve", lambda e, c=cv, p=pq: e.tensor_copy(c[:, 3:515], p[:, 0:512]),
                   reads=[rps], writes=[rcv], dbg=f"cve{ti}.{ct}")
                op("act", lambda e, o=ms, i=cv[:, 0:512], w=g.convw_sb[:, 4 * ct:4 * ct + 1]:
                   e.activation(o, i, AF.Identity, bias=0.0, scale=w[:, 0:1]),
                   reads=[rcv, "convw"], writes=[rm], dbg=f"cnv0.{ti}.{ct}")
                for j in (1, 3):
                    op("dve", lambda e, o=ms, i=cv[:, j:j + 512], w=g.convw_sb[:, 4 * ct + j:4 * ct + j + 1]:
                       e.scalar_tensor_tensor(o, i, w[:, 0:1], o, ALU.mult, ALU.add),
                       reads=[rcv, rm, "convw"], writes=[rm], dbg=f"cnv{j}.{ti}.{ct}")
                    if j == 1:
                        op("pool", lambda e, o=tt_, i=cv[:, 2:2 + 512],
                           w=g.convw_sb[:, 4 * ct + 2:4 * ct + 3]:
                           e.tensor_tensor(o, i, w.broadcast_to([128, 512]), ALU.mult),
                           reads=[rcv, "convw"], writes=[rt], dbg=f"cnv2a.{ti}.{ct}")
                        op("pool", lambda e, o=ms, a=tt_:
                           e.tensor_tensor(o, o, a, ALU.add),
                           reads=[rt, rm], writes=[rm], dbg=f"cnv2b.{ti}.{ct}")
                src_ap = ms
                rsrc = rm
            else:
                src_ap = pq[:, 0:512]
                rsrc = rps
            op("act", lambda e, o=tt_, i=src_ap: e.activation(o, i, AF.Tanh, bias=0.0, scale=0.5),
               reads=[rsrc], writes=[rt], dbg=f"th{ti}.{ct}", aset="T")
            if ct < 16:
                op("pool", lambda e, o=tt_, i=src_ap: e.tensor_tensor(o, i, o, ALU.mult),
                   reads=[rsrc, rt], writes=[rt])
                op("dve", lambda e, o=ms, i=src_ap, t2=tt_: e.tensor_tensor(o, i, t2, ALU.add),
                   reads=[rsrc, rt], writes=[rm], dbg=f"sl{ti}.{ct}")
                outbuf, rout = ms, rm
            else:
                op("act", lambda e, o=ms, i=src_ap: e.activation(o, i, AF.Identity),
                   reads=[rsrc], writes=[rm], dbg=f"zcp{ti}.{ct}")
                op("pool", lambda e, o=tt_, i=ms: e.tensor_tensor(o, i, o, ALU.mult),
                   reads=[rm, rt], writes=[rt])
                op("dve", lambda e, o=ms, i=ms, t2=tt_: e.tensor_tensor(o, i, t2, ALU.add),
                   reads=[rm, rt], writes=[rm], dbg=f"zsl{ti}.{ct}")
                outbuf, rout = ms, rm
            if ct < 8:
                isq = ct < 4
                hk = ct % 4
                stripo = o_qts[par] if isq else o_kts[par]
                srg = ("qts" if isq else "kts") + str(par)
                ssqt = g.ssqQ if isq else g.ssqK
                ssrg = ("ssqq" if isq else "ssqk") + str(par)
                for cc in range(CPT):
                    op("pe", lambda e, o=psb[0][:, 128 * cc:128 * cc + 128],
                       i=outbuf[:, 128 * cc:128 * cc + 128], idn=g.iden128[:, :]:
                       e.transpose(o, i, idn),
                       reads=[rout, "iden128"], writes=["psb0"], dbg=f"Tq{ti}.{ct}.{cc}")
                dst = ab(stripo, 2048).rearrange("p (c h e) -> p c h e", c=4, h=4)[:, :, hk, :]
                op("act" if ct % 2 == 0 else "dve",
                   lambda e, d=dst, p=psb[0][:, 0:512].rearrange("p (c e) -> p c e", c=4):
                   (e.activation(d, p, AF.Identity) if hasattr(e, "activation") else e.tensor_copy(d, p)),
                   reads=["psb0"], writes=[srg], dbg=f"Tqe{ti}.{ct}")
                for cc in range(CPT):
                    sl = ab(stripo, 2048)[:, (cc * 4 + hk) * 128:(cc * 4 + hk) * 128 + 128]
                    op("dve", lambda e, s=sl,
                       t=ssqt[:, 16 * par + cc * 4 + hk:16 * par + cc * 4 + hk + 1],
                       jk=ab(o_jk2, 128):
                       e.scalar_tensor_tensor(jk, s, 1.0, s, ALU.mult, ALU.mult, accum_out=t),
                       reads=[srg], writes=["jk2", ssrg], dbg=f"ssq{ti}.{ct}.{cc}")
            else:
                hv = (ct - 8) % 8
                isv = ct < 16
                dsto = o_vts if isv else o_zts
                pfx = "vts" if isv else "zts"
                for cc in range(CPT):
                    op("pe", lambda e, o=psb[1][:, 128 * cc:128 * cc + 128],
                       i=outbuf[:, 128 * cc:128 * cc + 128], idn=g.iden128[:, :]:
                       e.transpose(o, i, idn),
                       reads=[rout, "iden128"], writes=["psb1"], dbg=f"Tv{ti}.{ct}.{cc}")
                for cc2 in range(2):
                    dst = ab(dsto, 4096).rearrange("p (c h e) -> p c h e", c=4, h=8)[
                        :, 2 * cc2:2 * cc2 + 2, hv, :]
                    op("act" if ct % 2 == 0 else "dve",
                       lambda e, d=dst, p=psb[1][:, 256 * cc2:256 * cc2 + 256].rearrange(
                           "p (c e) -> p c e", c=2):
                       (e.activation(d, p, AF.Identity) if hasattr(e, "activation") else e.tensor_copy(d, p)),
                       reads=["psb1"], writes=[f"{pfx}{2 * cc2}", f"{pfx}{2 * cc2 + 1}"],
                       dbg=f"Tve{ti}.{ct}.{cc2}")
            yield
        if part == 0:
            return
        op("act", lambda e: e.activation(g.rsqQ[:, 16 * par:16 * par + 16],
                                         g.ssqQ[:, 16 * par:16 * par + 16], AF.Ln, bias=g.cstc[:, 4:5]),
           reads=[f"ssqq{par}", "cstc"], writes=[f"rsqq{par}"], dbg=f"lnq{ti}", aset="L")
        op("act", lambda e: e.activation(g.rsqQ[:, 16 * par:16 * par + 16],
                                         g.rsqQ[:, 16 * par:16 * par + 16], AF.Exp, bias=0.0, scale=-0.5),
           reads=[f"rsqq{par}"], writes=[f"rsqq{par}"])
        op("act", lambda e: e.activation(g.rsqK[:, 16 * par:16 * par + 16],
                                         g.ssqK[:, 16 * par:16 * par + 16], AF.Ln, bias=g.cstc[:, 4:5]),
           reads=[f"ssqk{par}", "cstc"], writes=[f"rsqk{par}"], dbg=f"lnk{ti}", aset="L")
        op("act", lambda e: e.activation(g.rsqK[:, 16 * par:16 * par + 16],
                                         g.rsqK[:, 16 * par:16 * par + 16], AF.Exp, bias=0.0, scale=-0.5),
           reads=[f"rsqk{par}"], writes=[f"rsqk{par}"])
        yield

    def finish_xg(tp):
        rst = g.rstd[tp % 2]
        rr = f"rstd{tp % 2}"
        rmq = f"msq{tp % 2}"
        rxg = "xg"
        op("act", lambda e, m=g.msq[tp % 2]: e.activation(rst[:, :], m[:, :], AF.Ln,
                                                          bias=g.cstc[:, 3:4], scale=1.0 / 128.0),
           reads=[rmq, "cstc"], writes=[rr], dbg=f"lnr{tp}", aset="L")
        op("act", lambda e: e.activation(rst[:, :], rst[:, :], AF.Exp, bias=0.0, scale=-0.5),
           reads=[rr], writes=[rr])
        xb_ = ab(o_xg, 4096)
        for cc2 in range(CPT):
            for hv in range(HVC):
                sl = xb_[:, 1024 * cc2 + 128 * hv:1024 * cc2 + 128 * hv + 128]
                rc = rst[:, 8 * cc2 + hv:8 * cc2 + hv + 1]
                if hv % 2:
                    op("pool", lambda e, s=sl, r=rc:
                       e.tensor_tensor(s, s, r.broadcast_to([128, 128]), ALU.mult),
                       reads=[rr, rxg], writes=[rxg], dbg=f"xgn{tp}.{cc2}.{hv}")
                else:
                    op("dve", lambda e, s=sl, r=rc:
                       e.tensor_scalar(s, s, r[:, 0:1], None, ALU.mult),
                       reads=[rr, rxg], writes=[rxg], dbg=f"xgn{tp}.{cc2}.{hv}")
        dma(nch("cs"), lambda e, t_=tp, x2=xb_: e.dma_start(
            out=g.xg_con[512 * t_:512 * t_ + 512, :].rearrange("(c s) n -> s c n", c=4),
            in_=x2.rearrange("p (c n) -> p c n", c=4)),
            reads=[rxg], writes=["xg_con"], dbg=f"stxg{tp}")

    def gen_body(ti):
        par0 = ti % 2
        for cc in range(CPT):
            c = CPT * ti + cc
            par = cc % 2
            for pfx, stripo, rsq, rrg, hoff, pboff in (
                    ("q", o_qts[par0], g.rsqQ, f"rsqq{par0}", o_qh, 0),
                    ("k", o_kts[par0], g.rsqK, f"rsqk{par0}", o_kh, 512)):
                strip_rg = pfx + f"ts{par0}"
                for hk in range(4):
                    sl = ab(stripo, 2048)[:, (cc * 4 + hk) * 128:(cc * 4 + hk) * 128 + 128]
                    rcol = rsq[:, 16 * par0 + cc * 4 + hk:16 * par0 + cc * 4 + hk + 1]
                    if hk % 2:
                        op("pool", lambda e, s=sl, r=rcol:
                           e.tensor_tensor(s, s, r.broadcast_to([128, 128]), ALU.mult),
                           reads=[rrg, strip_rg], writes=[strip_rg], dbg=f"nrm{pfx}{ti}.{cc}.{hk}")
                    else:
                        op("dve", lambda e, s=sl, r=rcol:
                           e.tensor_scalar(s, s, r[:, 0:1], None, ALU.mult),
                           reads=[rrg, strip_rg], writes=[strip_rg], dbg=f"nrm{pfx}{ti}.{cc}.{hk}")
                    op("pe", lambda e, o=psb[0][:, pboff + 128 * hk:pboff + 128 * hk + 128],
                       i=sl, idn=g.iden128[:, :]: e.transpose(o, i, idn),
                       reads=[strip_rg, "iden128"], writes=["psb0"], dbg=f"bT{pfx}{ti}.{cc}.{hk}")
                op("act", lambda e, d=ab(hoff[par], 512), p=psb[0][:, pboff:pboff + 512]:
                   e.activation(d, p, AF.Identity),
                   reads=["psb0"], writes=[pfx + f"hat{par}"], dbg=f"bTe{pfx}{ti}.{cc}")
            if g.dbg_qts is not None and ti == 0 and cc == 3:
                dma(nch("cs"), lambda e: e.dma_start(out=g.dbg_qts, in_=ab(o_qts[0], 2048)),
                    reads=["qts0"], writes=["dbgq"], dbg="dbgq")
                dma(nch("cs"), lambda e: e.dma_start(out=g.dbg_kts, in_=ab(o_kts[0], 2048)),
                    reads=["kts0"], writes=["dbgk"], dbg="dbgk")
            yield
            for hv in range(HVC):
                tid, gb, sl_ = _rowslot(hv)
                bank = 2 if hv < 4 else 3
                slot = psf[bank][:, 128 * (hv % 4):128 * (hv % 4) + 128]
                op("pe", lambda e, s=slot, l=g.selb[gb:gb + 4, 128 * sl_:128 * sl_ + 128],
                   r=g.rows2[tid][gb:gb + 4, 128 * c:128 * c + 128]:
                   e.matmul(s, l, r, start=True, stop=False),
                   reads=["selb", "rows2"], writes=[f"psf{bank}"], dbg=f"D1.{ti}.{cc}.{hv}")
                op("pe", lambda e, s=slot: e.matmul(s, g.iden128[:, :], g.mask128[:, :],
                                                    start=False, stop=True),
                   reads=["iden128", "mask128"], writes=[f"psf{bank}"], dbg=f"D3.{ti}.{cc}.{hv}")
                op("act", lambda e, s=slot, o=af(o_E[par], 1024)[:, 128 * hv:128 * hv + 128],
                   b=g.negG_sl[:, 8 * c + hv:8 * c + hv + 1]:
                   e.activation(o, s, AF.Exp, bias=b[:, 0:1], scale=1.0),
                   reads=[f"psf{bank}", "negG_sl"], writes=[f"E{par}"], dbg=f"E.{ti}.{cc}.{hv}")
            if g.dbg_E is not None and ti == 0:
                dma(nch("cs"), lambda e, c2=cc, p2=par: e.dma_start(
                    out=g.dbg_E[:, 1024 * c2:1024 * c2 + 1024], in_=af(o_E[p2], 1024)),
                    reads=[f"E{par}"], writes=["dbgE"], dbg=f"dbgE{cc}")
            for hk in range(4):
                bank = 4 if hk < 2 else 5
                pk = psf[bank][:, 256 * (hk % 2):256 * (hk % 2) + 256]
                for half, srcoff in ((0, o_qh[par]), (1, o_kh[par])):
                    op("pe", lambda e, p=pk[:, 128 * half:128 * half + 128],
                       l=ab(o_kh[par], 512)[:, 128 * hk:128 * hk + 128],
                       r=ab(srcoff, 512)[:, 128 * hk:128 * hk + 128]:
                       e.matmul(p, l, r, start=True, stop=True),
                       reads=[f"khat{par}", f"qhat{par}"], writes=[f"psf{bank}"],
                       dbg=f"P1.{ti}.{cc}.{hk}.{half}")
            yield
            for hv in range(HVC):
                hk = hv // 2
                pbank = 4 if hk < 2 else 5
                pk = psf[pbank][:, 256 * (hk % 2):256 * (hk % 2) + 256]
                pet = ab(o_PE, 2048)[:, 256 * hv:256 * hv + 256]
                ev = af(o_E[par], 1024)[:, 128 * hv:128 * hv + 128]
                op("dve", lambda e, o=pet, p=pk, e2=ev:
                   e.tensor_tensor(o.rearrange("s (r n) -> s r n", r=2),
                                   p.rearrange("s (r n) -> s r n", r=2),
                                   e2.unsqueeze(1).broadcast_to([128, 2, 128]), ALU.mult),
                   reads=[f"psf{pbank}", f"E{par}"], writes=["PEX"], dbg=f"PEt.{ti}.{cc}.{hv}")
                op("pool", lambda e, o=ab(o_A[par], 1024)[:, 128 * hv:128 * hv + 128],
                   i=pet[:, 0:128], b=g.betaS_sl[:, 8 * c + hv:8 * c + hv + 1]:
                   e.tensor_tensor(o, i, b.broadcast_to([128, 128]), ALU.mult),
                   reads=["PEX", "betaS_sl"], writes=[f"A{par}"], dbg=f"A.{ti}.{cc}.{hv}")
                op("dve", lambda e, o=ab(o_NT[0], 1024)[:, 128 * hv:128 * hv + 128],
                   i=pet[:, 128:256], b=g.beta_tok[:, 8 * c + hv:8 * c + hv + 1]:
                   e.scalar_tensor_tensor(o, i, b[:, 0:1], g.strict128[:, :], ALU.mult, ALU.mult),
                   reads=["PEX", "beta", "strict128"], writes=["NT_0"], dbg=f"NT0.{ti}.{cc}.{hv}")
            yield
            for hv in range(HVC):
                op("pe", lambda e, o=psb[1][:, 128 * hv:128 * hv + 128],
                   i=ab(o_NT[0], 1024)[:, 128 * hv:128 * hv + 128],
                   idn=g.iden128[:, :]: e.transpose(o, i, idn),
                   reads=["NT_0", "iden128"], writes=["psb1"], dbg=f"N0T.{ti}.{cc}.{hv}")
            op("dve", lambda e, d=ab(o_N[0], 1024), p=psb[1][:, :]: e.tensor_copy(d, p),
               reads=["psb1"], writes=["Nb0"], dbg=f"N0e.{ti}.{cc}")

            def sq_level(k):
                ntk, ntkr = nt_buf(k)
                ntk1, ntk1r = nt_buf(k + 1)
                for hv in range(HVC):
                    bank = 2 if hv < 4 else 3
                    op("pe", lambda e, p=psf[bank][:, 128 * (hv % 4):128 * (hv % 4) + 128],
                       l=ab(o_N[k % 2], 1024)[:, 128 * hv:128 * hv + 128],
                       r=ntk[:, 128 * hv:128 * hv + 128]:
                       e.matmul(p, l, r, start=True, stop=True),
                       reads=[f"Nb{k % 2}", ntkr], writes=[f"psf{bank}"], dbg=f"sqT{k}.{ti}.{cc}.{hv}")
                for b_ in (2, 3):
                    op("act" if k % 2 else "dve",
                       lambda e, d=ntk1[:, 512 * (b_ - 2):512 * (b_ - 2) + 512], p=psf[b_][:, :]:
                       (e.activation(d, p, AF.Identity) if hasattr(e, "activation") else e.tensor_copy(d, p)),
                       reads=[f"psf{b_}"], writes=[ntk1r], dbg=f"sqTe{k}.{b_}.{ti}.{cc}")
                if k < 5:
                    for hv in range(HVC):
                        bank = 4 if hv < 4 else 5
                        op("pe", lambda e, p=psf[bank][:, 128 * (hv % 4):128 * (hv % 4) + 128],
                           l=ntk[:, 128 * hv:128 * hv + 128],
                           r=ab(o_N[k % 2], 1024)[:, 128 * hv:128 * hv + 128]:
                           e.matmul(p, l, r, start=True, stop=True),
                           reads=[f"Nb{k % 2}", ntkr], writes=[f"psf{bank}"], dbg=f"sqN{k}.{ti}.{cc}.{hv}")
                    for b_ in (4, 5):
                        op("dve" if k % 2 else "act",
                           lambda e, d=ab(o_N[(k + 1) % 2], 1024)[:, 512 * (0 if b_ == 4 else 1):
                                                                  512 * (0 if b_ == 4 else 1) + 512],
                           p=psf[b_][:, :]:
                           (e.activation(d, p, AF.Identity) if hasattr(e, "activation") else e.tensor_copy(d, p)),
                           reads=[f"psf{b_}"], writes=[f"Nb{(k + 1) % 2}"], dbg=f"sqNe{k}.{b_}.{ti}.{cc}")
            for k in range(3):
                sq_level(k)
            yield
            for k in range(3, 6):
                sq_level(k)
            for hv in range(HVC):
                hk = hv // 2
                kc = af(o_kc, 32)[:, 8 * par + hv:8 * par + hv + 1]
                op("dve", lambda e, o=kc, b=g.beta_tok[:, 8 * c + hv:8 * c + hv + 1],
                   ec=af(o_E[par], 1024)[:, 128 * hv + 127:128 * hv + 128]:
                   e.tensor_tensor(o, b, ec, ALU.mult),
                   reads=["beta", f"E{par}"], writes=[f"kdc{par}"], dbg=f"kdc.{ti}.{cc}.{hv}")
                op("pool", lambda e, o=ab(o_kd[par], 1024)[:, 128 * hv:128 * hv + 128],
                   i=ab(o_kts[par0], 2048)[:, (cc * 4 + hk) * 128:(cc * 4 + hk) * 128 + 128],
                   s=kc: e.tensor_tensor(o, i, s.broadcast_to([128, 128]), ALU.mult),
                   reads=[f"kts{par0}", f"kdc{par}"], writes=[f"kd{par}"], dbg=f"kd.{ti}.{cc}.{hv}")
            yield
            for hv in range(HVC):
                hk = hv // 2
                bank = 4 if hv < 4 else 5
                op("pe", lambda e, p=psf[bank][:, 128 * (hv % 4):128 * (hv % 4) + 128],
                   l=ab(o_kh[par], 512)[:, 128 * hk:128 * hk + 128],
                   r=g.S_bf[:, 128 * hv:128 * hv + 128]:
                   e.matmul(p, l, r, start=True, stop=True),
                   reads=[f"khat{par}", f"Sb{hv}"], writes=[f"psf{bank}"], dbg=f"KS.{ti}.{cc}.{hv}")
            for hv in range(HVC):
                bank = 4 if hv < 4 else 5
                op("dve", lambda e, o=ab(o_yb[0], 1024)[:, 128 * hv:128 * hv + 128],
                   i=psf[bank][:, 128 * (hv % 4):128 * (hv % 4) + 128],
                   s=g.neG_sl[:, 8 * c + hv:8 * c + hv + 1],
                   v=ab(o_vts, 4096)[:, 1024 * cc + 128 * hv:1024 * cc + 128 * hv + 128]:
                   e.scalar_tensor_tensor(o, i, s[:, 0:1], v, ALU.mult, ALU.add),
                   reads=[f"psf{bank}", "neG_sl", f"vts{cc}"], writes=["yb0"], dbg=f"r.{ti}.{cc}.{hv}")
            if g.dbg_r is not None and ti == 0:
                dma(nch("cs"), lambda e, c2=cc: e.dma_start(
                    out=g.dbg_r[:, 1024 * c2:1024 * c2 + 1024], in_=ab(o_yb[0], 1024)),
                    reads=["yb0"], writes=["dbgr"], dbg=f"dbgr{cc}")
            yield
            for half8 in range(2):
                for l_ in ((0, 1, 2, 3) if half8 == 0 else (4, 5, 6)):
                    kk = 6 - l_
                    ntk, ntkr = nt_buf(kk)
                    use_act = (l_ % 2 == 1)
                    for hv in range(HVC):
                        bank = 2 if hv < 4 else 3
                        p_ = psf[bank][:, 128 * (hv % 4):128 * (hv % 4) + 128]
                        if use_act:
                            op("pe", lambda e, p=p_,
                               r=ab(o_yb[l_ % 2], 1024)[:, 128 * hv:128 * hv + 128]:
                               e.matmul(p, g.iden128[:, :], r, start=True, stop=False),
                               reads=["iden128", f"yb{l_ % 2}"], writes=[f"psf{bank}"],
                               dbg=f"apI{l_}.{ti}.{cc}.{hv}")
                        op("pe", lambda e, p=p_, l=ntk[:, 128 * hv:128 * hv + 128],
                           r=ab(o_yb[l_ % 2], 1024)[:, 128 * hv:128 * hv + 128],
                           s=(not use_act): e.matmul(p, l, r, start=s, stop=True),
                           reads=[ntkr, f"yb{l_ % 2}"], writes=[f"psf{bank}"],
                           dbg=f"ap{l_}.{ti}.{cc}.{hv}")
                    for b_ in (2, 3):
                        if use_act:
                            op("act", lambda e, o=ab(o_yb[(l_ + 1) % 2], 1024)[
                                :, 512 * (b_ - 2):512 * (b_ - 2) + 512], p=psf[b_][:, :]:
                               e.activation(o, p, AF.Identity),
                               reads=[f"psf{b_}"], writes=[f"yb{(l_ + 1) % 2}"],
                               dbg=f"apeA{l_}.{b_}.{ti}.{cc}")
                        else:
                            op("dve", lambda e, o=ab(o_yb[(l_ + 1) % 2], 1024)[
                                :, 512 * (b_ - 2):512 * (b_ - 2) + 512],
                               a=ab(o_yb[l_ % 2], 1024)[:, 512 * (b_ - 2):512 * (b_ - 2) + 512],
                               p=psf[b_][:, :]: e.tensor_tensor(o, a, p, ALU.add),
                               reads=[f"psf{b_}", f"yb{l_ % 2}"], writes=[f"yb{(l_ + 1) % 2}"],
                               dbg=f"ape{l_}.{b_}.{ti}.{cc}")
                yield
            for hv in range(HVC):
                bank = 2 if hv < 4 else 3
                op("pe", lambda e, p=psf[bank][:, 128 * (hv % 4):128 * (hv % 4) + 128],
                   l=ab(o_A[par], 1024)[:, 128 * hv:128 * hv + 128],
                   r=ab(o_yb[1], 1024)[:, 128 * hv:128 * hv + 128]:
                   e.matmul(p, l, r, start=True, stop=True),
                   reads=[f"A{par}", "yb1"], writes=[f"psf{bank}"], dbg=f"Au.{ti}.{cc}.{hv}")
            for hv in range(HVC):
                hk = hv // 2
                bank = 4 if hv < 4 else 5
                op("pe", lambda e, p=psf[bank][:, 128 * (hv % 4):128 * (hv % 4) + 128],
                   l=ab(o_qh[par], 512)[:, 128 * hk:128 * hk + 128],
                   r=g.S_bf[:, 128 * hv:128 * hv + 128]:
                   e.matmul(p, l, r, start=True, stop=True),
                   reads=[f"qhat{par}", f"Sb{hv}"], writes=[f"psf{bank}"], dbg=f"QS.{ti}.{cc}.{hv}")
            for hv in range(HVC):
                bank = 4 if hv < 4 else 5
                op("act", lambda e, o=af(o_oq, 1024)[:, 128 * hv:128 * hv + 128],
                   i=psf[bank][:, 128 * (hv % 4):128 * (hv % 4) + 128],
                   s=g.eG8_sl[:, 8 * c + hv:8 * c + hv + 1]:
                   e.activation(o, i, AF.Identity, bias=0.0, scale=s[:, 0:1]),
                   reads=[f"psf{bank}", "eG8_sl"], writes=["oq"], dbg=f"oq.{ti}.{cc}.{hv}")
            for b_ in (2, 3):
                op("dve", lambda e, o=af(o_ob, 1024)[:, 512 * (b_ - 2):512 * (b_ - 2) + 512],
                   a=af(o_oq, 1024)[:, 512 * (b_ - 2):512 * (b_ - 2) + 512],
                   p=psf[b_][:, :]: e.tensor_tensor(o, a, p, ALU.add),
                   reads=[f"psf{b_}", "oq"], writes=["ob"], dbg=f"o.{b_}.{ti}.{cc}")
            if g.dbg_ob is not None and ti == 0:
                dma(nch("cs"), lambda e, c2=cc: e.dma_start(
                    out=g.dbg_ob[:, 1024 * c2:1024 * c2 + 1024], in_=af(o_ob, 1024)),
                    reads=["ob"], writes=["dbgo"], dbg=f"dbgo{cc}")
                dma(nch("cs"), lambda e, c2=cc: e.dma_start(
                    out=g.dbg_u[:, 1024 * c2:1024 * c2 + 1024], in_=ab(o_yb[1], 1024)),
                    reads=["yb1"], writes=["dbgu"], dbg=f"dbgu{cc}")
            for hv in range(HVC):
                bank = 2 if hv < 4 else 3
                op("pe", lambda e, p=psf[bank][:, 128 * (hv % 4):128 * (hv % 4) + 128],
                   l=ab(o_kd[par], 1024)[:, 128 * hv:128 * hv + 128],
                   r=ab(o_yb[1], 1024)[:, 128 * hv:128 * hv + 128]:
                   e.matmul(p, l, r, start=True, stop=True),
                   reads=[f"kd{par}", "yb1"], writes=[f"psf{bank}"], dbg=f"SU.{ti}.{cc}.{hv}")
            for hv in range(HVC):
                bank = 2 if hv < 4 else 3
                op("dve", lambda e, o=g.S_sb[:, 128 * hv:128 * hv + 128],
                   s=g.egC_all[:, 8 * c + hv:8 * c + hv + 1],
                   p=psf[bank][:, 128 * (hv % 4):128 * (hv % 4) + 128]:
                   e.scalar_tensor_tensor(o, o, s[:, 0:1], p, ALU.mult, ALU.add),
                   reads=[f"Ss{hv}", "egC", f"psf{bank}"], writes=[f"Ss{hv}"],
                   dbg=f"Sup.{ti}.{cc}.{hv}")
                op("pool", lambda e, o=g.S_bf[:, 128 * hv:128 * hv + 128],
                   i=g.S_sb[:, 128 * hv:128 * hv + 128]: e.tensor_copy(o, i),
                   reads=[f"Ss{hv}"], writes=[f"Sb{hv}"], dbg=f"Sbf.{ti}.{cc}.{hv}")
            if g.dbg_S is not None and ti == 0 and cc == 0:
                dma(nch("cs"), lambda e: e.dma_start(out=g.dbg_vts, in_=ab(o_vts, 4096)),
                    reads=[f"vts{c2}" for c2 in range(4)], writes=["dbgv"], dbg="dbgv")
                dma(nch("cs"), lambda e: e.dma_start(out=g.dbg_zts, in_=ab(o_zts, 4096)),
                    reads=[f"zts{c2}" for c2 in range(4)], writes=["dbgz"], dbg="dbgz")
                dma(nch("cs"), lambda e: e.dma_start(out=g.dbg_S, in_=g.S_bf[:, :]),
                    reads=[f"Sb{hv2}" for hv2 in range(8)], writes=["dbgS"], dbg="dbgS")
                dma(nch("cs"), lambda e: e.dma_start(out=g.dbg_Ss, in_=g.S_sb[:, :]),
                    reads=[f"Ss{hv2}" for hv2 in range(8)], writes=["dbgSs"], dbg="dbgSs")
                dma(nch("cs"), lambda e: e.dma_start(out=g.dbg_kd, in_=ab(o_kd[0], 1024)),
                    reads=["kd0"], writes=["dbgkd"], dbg="dbgkd")
            op("pool", lambda e, o=ab(o_xg, 4096)[:, 1024 * cc:1024 * cc + 1024],
               a=af(o_ob, 1024), z=ab(o_zts, 4096)[:, 1024 * cc:1024 * cc + 1024]:
               e.tensor_tensor(o, a, z, ALU.mult),
               reads=["ob", f"zts{cc}"], writes=["xg"], dbg=f"xg0.{ti}.{cc}")
            op("act", lambda e, o=ab(o_jk, 1024), i=ab(o_xg, 4096)[:, 1024 * cc:1024 * cc + 1024]:
               e.activation(o, i, AF.Square),
               reads=["xg"], writes=["jk"], dbg=f"sq.{ti}.{cc}")
            op("dve", lambda e, o=g.msq[ti % 2][:, 8 * cc:8 * cc + 8].unsqueeze(2),
               i=ab(o_jk, 1024).rearrange("p (j e) -> p j e", j=8):
               e.tensor_reduce(o, i, AX.X, ALU.add),
               reads=["jk"], writes=[f"msq{ti % 2}"], dbg=f"msq.{ti}.{cc}")
            yield
        finish_xg(ti)
        yield

    def drain(gen):
        for _ in gen:
            pass

    heads_qk = [gen_head(t, 0) for t in range(NTT)]
    heads_vz = [gen_head(t, 1) for t in range(NTT)]
    bodies = [gen_body(t) for t in range(NTT)]
    drain(heads_qk[0])
    drain(heads_vz[0])
    for ti in range(NTT):
        h = heads_qk[ti + 1] if ti + 1 < NTT else None
        b = bodies[ti]
        while True:
            done_b = next(b, StopIteration) is StopIteration
            if h is not None:
                next(h, None)
                if done_b:
                    drain(h)
            if done_b:
                break
        if ti + 1 < NTT:
            drain(heads_vz[ti + 1])
    dma(g.cc2, lambda e: e.collective_compute(
        "AllGather", ALU.bypass,
        replica_groups=[[0, 1, 2, 3], [4, 5, 6, 7]],
        ins=[g.xg_con.opt()], outs=[g.xg_all.opt()]),
        reads=["xg_con"], writes=["xg_all"], dbg="AG2", cost=90000)

    sched.barrier(keep=("xg_all",), exclude=(g.cc2,))
    cab = _Carve(AB.shape[1])
    caf = _Carve(AFW.shape[1])
    o_xnl2 = cab.take(16 * TSHARD)
    o_wgt = [cab.take(2048) for _ in range(4)]
    o_wdh = [cab.take(11264), cab.take(11264)]
    o_wdf = o_wdh[0]
    o_h = cab.take(11264)
    o_ft = [cab.take(512), cab.take(512)]
    o_fb = [cab.take(512), cab.take(512)]
    o_ys = [caf.take(2048), caf.take(2048)]

    xnl2 = AB[:, o_xnl2:o_xnl2 + 16 * TSHARD]
    dma(nch("cs"), lambda e: e.dma_start(out=xnl2.rearrange("p (c t) -> p c t", c=16),
                                         in_=g.xn_con.rearrange("(c p) t -> p c t", p=128)),
        reads=["xn_con"], writes=["xnl2"], dbg="ldxnl2", cost=12000)

    _guctr = [0]

    def gu_evac(pg, pu, rgps, rups, hdst, rh, tag):
        _guctr[0] += 1
        par2 = _guctr[0] % 2
        ft = ab(o_ft[par2], 512)
        fb = ab(o_fb[par2], 512)
        rft, rfb = f"ft{par2}", f"fb{par2}"
        op("act", lambda e: e.activation(ft, pg, AF.Tanh, bias=0.0, scale=0.5),
           reads=[rgps], writes=[rft], dbg="t" + tag)
        op("dve", lambda e: e.scalar_tensor_tensor(ft, pg, 0.5, ft, ALU.mult, ALU.mult),
           reads=[rgps, rft], writes=[rft])
        op("dve", lambda e: e.scalar_tensor_tensor(fb, pg, 0.5, ft, ALU.mult, ALU.add),
           reads=[rgps, rft], writes=[rfb])
        op("dve", lambda e: e.tensor_tensor(hdst, fb, pu, ALU.mult),
           reads=[rfb, rups], writes=[rh], dbg="h" + tag)

    def ffn_expert(ei):
        for ht in range(11):
            wgt = ab(o_wgt[ht % 2], 2048)
            wut = ab(o_wgt[2 + ht % 2], 2048)
            rg_, ru_ = f"wgt{ht % 2}", f"wut{ht % 2}"
            dma(nch("cw"), lambda e, d=wgt.rearrange("p (c n) -> p c n", c=16), h_=ht: e.dma_start(
                out=d, in_=g.wg_d[ei, :, 128 * h_:128 * h_ + 128].rearrange("(c p) n -> p c n", p=128)),
                reads=[], writes=[rg_], dbg=f"lwg{ei}.{ht}", cost=3000)
            dma(nch("cw"), lambda e, d=wut.rearrange("p (c n) -> p c n", c=16), h_=ht: e.dma_start(
                out=d, in_=g.wu_d[ei, :, 128 * h_:128 * h_ + 128].rearrange("(c p) n -> p c n", p=128)),
                reads=[], writes=[ru_], dbg=f"lwu{ei}.{ht}", cost=3000)
            for j2 in range(2):
                bk = 2 * (ht % 2)
                for which, wt, rw_, pix in (("g", wgt, rg_, bk), ("u", wut, ru_, bk + 1)):
                    for d16 in range(16):
                        op("pe", lambda e, p=psf[pix][:, 0:512], l=wt[:, 128 * d16:128 * d16 + 128],
                           r=xnl2.rearrange("p (c t) -> p c t", c=16)[:, d16, 512 * j2:512 * j2 + 512],
                           s=(d16 == 0), t=(d16 == 15): e.matmul(p, l, r, start=s, stop=t),
                           reads=[rw_, "xnl2"], writes=[f"psf{pix}"],
                           dbg=f"gu{ei}.{ht}.{j2}.{which}.{d16}", cost=230)
                gu_evac(psf[bk][:, 0:512], psf[bk + 1][:, 0:512], f"psf{bk}", f"psf{bk + 1}",
                        AB[:, o_h + 1024 * ht + 512 * j2:o_h + 1024 * ht + 512 * j2 + 512],
                        f"h{ht}", f"{ei}.{ht}.{j2}")
        for dh in range(2):
            for sp_ in range(2):
                a0, a1 = (0, 6) if sp_ == 0 else (6, 11)
                dma(nch("cw"), lambda e, dh_=dh, a0_=a0, a1_=a1: e.dma_start(
                    out=AB[:, o_wdh[dh_]:o_wdh[dh_] + 11264].rearrange("p (a n) -> p a n", a=11)[:, a0_:a1_, :],
                    in_=g.wd_d[ei].rearrange("(a p) n -> p a n", p=128)[:, a0_:a1_, 1024 * dh_:1024 * dh_ + 1024]),
                    reads=[], writes=[f"wd{dh}s{sp_}"], dbg=f"lwd{ei}.{dh}.{sp_}", cost=8000)
        for dh in range(2):
            for ts_ in range(8):
                yi = 8 * dh + ts_
                ys = af(o_ys[yi % 2], 2048)[:, 0:1024]
                rys = f"ys{yi % 2}"
                for dt in range(2):
                    pd = psf[4 + dt % 2]
                    rpd = f"psf{4 + dt % 2}"
                    for ht in range(11):
                        op("pe", lambda e, p=pd[:, 0:512],
                           l=AB[:, o_h + 1024 * ht + 128 * ts_:o_h + 1024 * ht + 128 * ts_ + 128],
                           r=AB[:, o_wdh[dh] + 1024 * ht + 512 * dt:o_wdh[dh] + 1024 * ht + 512 * dt + 512],
                           s=(ht == 0), t=(ht == 10): e.matmul(p, l, r, start=s, stop=t),
                           reads=[f"h{ht}", f"wd{dh}s{0 if ht < 6 else 1}"], writes=[rpd],
                           dbg=f"dn{ei}.{dh}.{ts_}.{dt}.{ht}", cost=230)
                    op("dve", lambda e, o=ys[:, 512 * dt:512 * dt + 512], p=pd[:, 0:512],
                       s=g.gates_sb[:, 8 * ts_ + ei:8 * ts_ + ei + 1]:
                       e.tensor_scalar(o, p, s[:, 0:1], None, ALU.mult),
                       reads=[rpd, "gates"], writes=[rys], dbg=f"dne{ei}.{dh}.{ts_}.{dt}")
                dma(nch("cy"), lambda e, t_=ts_, dh_=dh, y2=ys, acc=(ei > 0): e.dma_start(
                    out=g.y_dram[128 * t_:128 * t_ + 128, 1024 * dh_:1024 * dh_ + 1024], in_=y2,
                    accum_op=(ALU.add if acc else ALU.bypass)),
                    reads=[rys], writes=[f"y{ts_}.{dh}"], dbg=f"ydma{ei}.{dh}.{ts_}", cost=1500)

    for ei in range(E_EXP):
        ffn_expert(ei)

    sched.barrier()
    h_sh = nc.dram_tensor("h_sh", [HSH, TSHARD], BF16).ap()
    for ht in range(22):
        wgt = ab(o_wgt[ht % 2], 2048)
        wut = ab(o_wgt[2 + ht % 2], 2048)
        rg_, ru_ = f"wgt{ht % 2}", f"wut{ht % 2}"
        dma(nch("cw"), lambda e, d=wgt.rearrange("p (c n) -> p c n", c=16), h_=ht: e.dma_start(
            out=d, in_=g.sg_d[:, 128 * h_:128 * h_ + 128].rearrange("(c p) n -> p c n", p=128)),
            reads=[], writes=[rg_], dbg=f"lsg{ht}", cost=3000)
        dma(nch("cw"), lambda e, d=wut.rearrange("p (c n) -> p c n", c=16), h_=ht: e.dma_start(
            out=d, in_=g.su_d[:, 128 * h_:128 * h_ + 128].rearrange("(c p) n -> p c n", p=128)),
            reads=[], writes=[ru_], dbg=f"lsu{ht}", cost=3000)
        for j2 in range(2):
            bk = 2 * (ht % 2)
            for which, wt, rw_, pix in (("g", wgt, rg_, bk), ("u", wut, ru_, bk + 1)):
                for d16 in range(16):
                    op("pe", lambda e, p=psf[pix][:, 0:512], l=wt[:, 128 * d16:128 * d16 + 128],
                       r=xnl2.rearrange("p (c t) -> p c t", c=16)[:, d16, 512 * j2:512 * j2 + 512],
                       s=(d16 == 0), t=(d16 == 15): e.matmul(p, l, r, start=s, stop=t),
                       reads=[rw_, "xnl2"], writes=[f"psf{pix}"],
                       dbg=f"sgu{ht}.{j2}.{which}.{d16}", cost=230)
            gu_evac(psf[bk][:, 0:512], psf[bk + 1][:, 0:512], f"psf{bk}", f"psf{bk + 1}",
                    AB[:, o_h + 512 * (j2):o_h + 512 * (j2) + 512], "hsh", f"s.{ht}.{j2}")
        dma(nch("cs"), lambda e, h_=ht: e.dma_start(out=h_sh[128 * h_:128 * h_ + 128, :],
                                                    in_=AB[:, o_h:o_h + 1024]),
            reads=["hsh"], writes=["h_sh"], dbg=f"sths{ht}")
    for half in range(2):
        for sp_ in range(2):
            a0, a1 = (0, 11) if sp_ == 0 else (11, 22)
            dma(nch("cw"), lambda e, hf=half, a0_=a0, a1_=a1: e.dma_start(
                out=AB[:, o_wdf:o_wdf + 22528].rearrange("p (a n) -> p a n", a=22)[:, a0_:a1_, :],
                in_=g.sd_d[:, 1024 * hf:1024 * hf + 1024].rearrange("(a p) n -> p a n", p=128)[:, a0_:a1_, :]),
                reads=[], writes=[f"wdfs{sp_}"], dbg=f"lsd{half}.{sp_}", cost=16000)
        for ts_ in range(8):
            htb = ab(o_h + 4096, 2816)
            for sp_ in range(2):
                a0, a1 = (0, 11) if sp_ == 0 else (11, 22)
                dma(nch("cs"), lambda e, t_=ts_, a0_=a0, a1_=a1,
                    d=htb.rearrange("p (a t) -> p a t", a=22): e.dma_start(
                    out=d[:, a0_:a1_, :],
                    in_=h_sh[:, 128 * t_:128 * t_ + 128].rearrange("(a p) t -> p a t", p=128)[:, a0_:a1_, :]),
                    reads=["h_sh"], writes=[f"htb{sp_}"], dbg=f"ldhs{half}.{ts_}.{sp_}")
            ys = af(o_ys[ts_ % 2], 2048)[:, 0:1024]
            rys = f"ys{ts_ % 2}"
            for dt in range(2):
                pd = psf[4 + dt % 2]
                rpd = f"psf{4 + dt % 2}"
                for ht in range(22):
                    op("pe", lambda e, p=pd[:, 0:512],
                       l=htb[:, 128 * ht:128 * ht + 128],
                       r=AB[:, o_wdf + 1024 * ht + 512 * dt:o_wdf + 1024 * ht + 512 * dt + 512],
                       s=(ht == 0), t=(ht == 21): e.matmul(p, l, r, start=s, stop=t),
                       reads=[f"htb{0 if ht < 11 else 1}", f"wdfs{0 if ht < 11 else 1}"], writes=[rpd],
                       dbg=f"sdn{half}.{ts_}.{dt}.{ht}", cost=230)
                op("dve", lambda e, o=ys[:, 512 * dt:512 * dt + 512], p=pd[:, 0:512]:
                   e.tensor_copy(o, p),
                   reads=[rpd], writes=[rys], dbg=f"sdne{half}.{ts_}.{dt}")
            dma(nch("cy"), lambda e, t_=ts_, hf=half, y2=ys: e.dma_start(
                out=g.y_dram[128 * t_:128 * t_ + 128, 1024 * hf:1024 * hf + 1024], in_=y2,
                accum_op=ALU.add),
                reads=[rys], writes=[f"y{ts_}"], dbg=f"sydma{half}.{ts_}")

    sched.barrier(keep=("xg_all",), exclude=(g.cc2,))
    cab = _Carve(AB.shape[1])
    o_wo = cab.take(32768)
    o_xgt = [cab.take(4096), cab.take(4096)]
    o_xgT = [cab.take(4096), cab.take(4096)]
    op("pool", lambda e: (e.cc_rank_ld(g.rank_reg, [[0, 1, 2, 3], [4, 5, 6, 7]]),
                          e.reg_mul(g.roff_reg, g.rank_reg, TSHARD * TSHARD))[-1],
       reads=[], writes=["roff"], dbg="rank")
    for wh in range(2):
        dma(nch("cw"), lambda e, hf=wh: e.dma_start(
            out=AB[:, o_wo:o_wo + 32768].rearrange("p (a n) -> p a n", a=32),
            in_=g.wout_d[:, 1024 * hf:1024 * hf + 1024].rearrange("(a p) n -> p a n", p=128)),
            reads=["wo"], writes=["wo"], dbg=f"lwo{wh}", cost=46000)
        op("dve", lambda e: e.tensor_scalar(AB[:, o_wo:o_wo + 32768], AB[:, o_wo:o_wo + 32768],
                                            g.normw_sb[:, 0:1], None, ALU.mult),
           reads=["wo", "normw"], writes=["wo"], dbg=f"nwo{wh}")
        for ts_ in range(8):
            tp2 = ts_ % 2
            xgt = ab(o_xgt[tp2], 4096)
            rxgt = f"xgt{tp2}"
            rxgT = f"xgT{tp2}"

            def _ldxg(e, t_=ts_, d=xgt.rearrange("p (j e2) -> p j e2", j=4)):
                e.reg_add(g.tmp_reg, g.roff_reg, (128 * t_) * 1024)
                return e.dma_start(out=d, in_=bass.AP(g.xg_all_h, g.tmp_reg,
                                                      [[1024, 128], [4096 * 1024, 4], [1, 1024]]))
            dma(nch("cx"), _ldxg, reads=["xg_all", "roff"], writes=[rxgt], dbg=f"lxg{wh}.{ts_}")
            for ch in range(32):
                op("pe", lambda e, o=psb[0][:, 128 * (ch % 8):128 * (ch % 8) + 128],
                   i=xgt[:, 128 * ch:128 * ch + 128]:
                   e.transpose(o, i, g.iden128[:, :]),
                   reads=[rxgt, "iden128"], writes=["psb0"], dbg=f"Txg{wh}.{ts_}.{ch}")
                if ch % 8 == 7:
                    op("dve", lambda e, d=ab(o_xgT[tp2] + 1024 * (ch // 8), 1024),
                       p=psb[0][:, 0:1024]: e.tensor_copy(d, p),
                       reads=["psb0"], writes=[rxgT], dbg=f"Texg{wh}.{ts_}.{ch // 8}")
            ys = af(o_ys[tp2], 2048)[:, 0:1024]
            rys = f"ys{tp2}"
            for dt in range(2):
                pd = psf[dt % 2]
                rpd = f"psf{dt % 2}"
                for ch in range(32):
                    op("pe", lambda e, p=pd[:, 0:512],
                       l=ab(o_xgT[tp2], 4096)[:, 128 * ch:128 * ch + 128],
                       r=AB[:, o_wo + 1024 * ch + 512 * dt:o_wo + 1024 * ch + 512 * dt + 512],
                       s=(ch == 0), t=(ch == 31): e.matmul(p, l, r, start=s, stop=t),
                       reads=[rxgT, "wo"], writes=[rpd], dbg=f"wo{wh}.{ts_}.{dt}.{ch}", cost=230)
                op("dve", lambda e, o=ys[:, 512 * dt:512 * dt + 512], p=pd[:, 0:512]:
                   e.tensor_copy(o, p),
                   reads=[rpd], writes=[rys], dbg=f"woe{wh}.{ts_}.{dt}")
            dma(nch("cy"), lambda e, t_=ts_, hf=wh, d=ys: e.dma_start(
                out=d, in_=g.x_d[128 * t_:128 * t_ + 128, 1024 * hf:1024 * hf + 1024],
                accum_op=ALU.add),
                reads=[rys], writes=[rys], dbg=f"wax{wh}.{ts_}")
            dma(nch("cy"), lambda e, t_=ts_, hf=wh, d=ys: e.dma_start(
                out=d, in_=g.y_dram[128 * t_:128 * t_ + 128, 1024 * hf:1024 * hf + 1024],
                accum_op=ALU.add),
                reads=[rys, f"y{ts_}"], writes=[rys], dbg=f"way{wh}.{ts_}")
            dma(nch("cy"), lambda e, t_=ts_, hf=wh, s=ys: e.dma_start(
                out=g.out_d[128 * t_:128 * t_ + 128, 1024 * hf:1024 * hf + 1024], in_=s),
                reads=[rys], writes=["outd"], dbg=f"fout{wh}.{ts_}")
    sched.barrier()



_CACHE = {}


def _consts():
    iden128 = np.eye(128, dtype=np.float32)
    ones = np.ones((128, 128), dtype=np.float32)
    sel = np.zeros((128, 256), dtype=np.float32)
    p = np.arange(128)
    sel[:, 0:128] = ((p % 4) < 2).astype(np.float32)[:, None]
    sel[:, 128:256] = ((p % 4) >= 2).astype(np.float32)[:, None]
    s128 = np.arange(128)[:, None]
    t128 = np.arange(128)[None, :]
    tril128 = (s128 <= t128).astype(np.float32)
    mask128 = np.where(s128 <= t128, 0.0, NEG_BIG).astype(np.float32)
    strict128 = np.where(s128 < t128, -1.0, 0.0).astype(np.float32)
    return dict(c_iden128=iden128, c_ones=ones, c_sel=sel,
                c_tril128=tril128, c_mask128=mask128, c_strict128=strict128,
                c_bias=np.tile(np.array([[1e-5, 1e-6, 1.0, 1.6e-5, 4e-6]], dtype=np.float32), (128, 1)))


def kernel(L0_attn_hidden_states, L0_attn_W_qkvz, L0_attn_W_ba, L0_attn_conv_w,
           L0_attn_dt_bias, L0_attn_A_log, L0_attn_norm_weight, L0_attn_W_out,
           L0_ffn_router_w, L0_ffn_wg, L0_ffn_wu, L0_ffn_wd, L0_ffn_sg,
           L0_ffn_su, L0_ffn_sd):
    x = np.asarray(L0_attn_hidden_states, dtype=np.float32)
    wq = np.asarray(L0_attn_W_qkvz, dtype=np.float32)
    wba = np.asarray(L0_attn_W_ba, dtype=np.float32)
    cw = np.asarray(L0_attn_conv_w, dtype=np.float32)
    dtb = np.asarray(L0_attn_dt_bias, dtype=np.float32)
    alog = np.asarray(L0_attn_A_log, dtype=np.float32)
    nw = np.asarray(L0_attn_norm_weight, dtype=np.float32)
    wout = np.asarray(L0_attn_W_out, dtype=np.float32)
    rw = np.asarray(L0_ffn_router_w, dtype=np.float32)
    wg = np.asarray(L0_ffn_wg, dtype=np.float32)
    wu = np.asarray(L0_ffn_wu, dtype=np.float32)
    wd = np.asarray(L0_ffn_wd, dtype=np.float32)
    sg = np.asarray(L0_ffn_sg, dtype=np.float32)
    su = np.asarray(L0_ffn_su, dtype=np.float32)
    sd = np.asarray(L0_ffn_sd, dtype=np.float32)

    if "nc" not in _CACHE:
        _CACHE["nc"] = build_program()
    nc, sched = _CACHE["nc"]

    consts = _consts()
    in_maps = []
    for c in range(8):
        b, r = c // 4, c % 4
        qs, ks, vs, zs = [], [], [], []
        for j in range(4):
            base = 768 * (4 * r + j)
            qs.append(wq[:, base:base + 128])
            ks.append(wq[:, base + 128:base + 256])
            vs.append(wq[:, base + 256:base + 512])
            zs.append(wq[:, base + 512:base + 768])
        wq_sl = np.ascontiguousarray(np.concatenate(qs + ks + vs + zs, axis=1))
        cw_rows = np.concatenate([cw[512 * r:512 * r + 512],
                                  cw[2048 + 512 * r:2048 + 512 * r + 512],
                                  cw[4096 + 1024 * r:4096 + 1024 * r + 1024]], axis=0)
        convw_sl = np.ascontiguousarray(cw_rows.reshape(16, 128, 4).transpose(1, 0, 2).reshape(128, 64))
        m = dict(consts)
        m.update(
            x_sh=np.ascontiguousarray(x[b, TSHARD * r:TSHARD * r + TSHARD, :]),
            wq_sl=wq_sl,
            wba_sl=np.ascontiguousarray(wba[:, 16 * r:16 * r + 16]),
            convw_sl=convw_sl,
            dtb_sl=np.ascontiguousarray(dtb[8 * r:8 * r + 8].reshape(1, 8)),
            alog_sl=np.ascontiguousarray(alog[8 * r:8 * r + 8].reshape(1, 8)),
            normw=np.ascontiguousarray(nw.reshape(128, 1)),
            wout=wout, router_w=rw, wg=wg, wu=wu, wd=wd, sg=sg, su=su, sd=sd,
        )
        in_maps.append(m)

    res = run_bass_kernel_spmd(nc, in_maps, core_ids=list(range(8)))
    out = np.empty((B, T, D), dtype=np.float32)
    for c in range(8):
        b, r = c // 4, c % 4
        out[b, TSHARD * r:TSHARD * r + TSHARD, :] = res.results[c]["out_sh"]
    return out
